# version 23
# baseline (speedup 1.0000x reference)
import numpy as np
import concourse.bass as bass
import concourse.mybir as mybir
from concourse.bass_utils import run_bass_kernel_spmd
from contextlib import ExitStack

F32 = mybir.dt.float32
BF16 = mybir.dt.bfloat16
AX = mybir.AxisListType
OP = mybir.AluOpType
ACT = mybir.ActivationFunctionType

S, D, NT, NF, NE, HD = 2048, 1024, 16, 8, 8, 128
DEBUG = False
TAPS = None
EXP_SHIFT = -16.0
EPS = 1e-5
KB = 1024
ARENA_BYTES = 212832
ENGINES = ("sync", "tensor", "vector", "scalar", "gpsimd")
OWN_SEM = {"tensor": "sPE", "vector": "sVE", "scalar": "sSC"}


class Sched:
    def __init__(self):
        self.ops = {e: [] for e in ENGINES}
        self.cnt = {}

    def sem(self, name):
        self.cnt[name] = 0

    def emit(self, eng, fn, waits=(), inc=None):
        waits = [(s, v) for (s, v) in waits if v > 0]
        for s, v in waits:
            assert v <= self.cnt[s], f"wait on future event {s}={v} (now {self.cnt[s]})"
        if inc is not None:
            self.cnt[inc[0]] += inc[1]
        self.ops[eng].append((tuple(waits), fn, inc))
        return self.cnt[inc[0]] if inc is not None else None


def run_engine(eng_obj, eng_name, ops, sems):
    own = OWN_SEM.get(eng_name)
    watermark = {}
    for waits, fn, inc in ops:
        for s, v in waits:
            if s == own or watermark.get(s, 0) >= v:
                continue
            eng_obj.wait_ge(sems[s], v)
            watermark[s] = v
        ins = fn(eng_obj)
        if inc is not None and ins is not None:
            ins.then_inc(sems[inc[0]], inc[1])


def build_nc():
    nc = bass.Bass("TRN2", target_bir_lowering=False, debug=False,
                   detect_race_conditions=False)

    x_d = nc.dram_tensor("x", [S, D], F32, kind="ExternalInput").ap()
    wqj_d = nc.dram_tensor("wqj", [NF, D, 128], F32, kind="ExternalInput").ap()
    wkj_d = nc.dram_tensor("wkj", [NF, D, 128], F32, kind="ExternalInput").ap()
    wv_d = nc.dram_tensor("wv", [D, D], F32, kind="ExternalInput").ap()
    w1_d = nc.dram_tensor("w1l", [128, NE, NF, HD], F32, kind="ExternalInput").ap()
    w2_d = nc.dram_tensor("w2l", [HD, NE, D], F32, kind="ExternalInput").ap()
    wg_d = nc.dram_tensor("wgl", [128, NF, NE], F32, kind="ExternalInput").ap()
    bq_d = nc.dram_tensor("bq2", [128, NF], F32, kind="ExternalInput").ap()
    bk_d = nc.dram_tensor("bk2", [128, NF], F32, kind="ExternalInput").ap()
    bv_d = nc.dram_tensor("bv2", [128, NF], F32, kind="ExternalInput").ap()
    b1_d = nc.dram_tensor("b12", [HD, NE], F32, kind="ExternalInput").ap()
    b2_d = nc.dram_tensor("b2l", [NE, D], F32, kind="ExternalInput").ap()
    bg_d = nc.dram_tensor("bg2", [NE, 1], F32, kind="ExternalInput").ap()
    out_d = nc.dram_tensor("out", [S, D], F32, kind="ExternalOutput").ap()
    dbg = {}
    if DEBUG:
        BD = mybir.dt.bfloat16
        for nm, shape, dt_ in [("d_ln1fm", [128, 16384], BD), ("d_q", [128, 2048], BD),
                               ("d_k", [128, 2048], BD), ("d_v", [128, 16384], BD),
                               ("d_pt0", [128, 1024], BD), ("d_att", [128, 16384], BD),
                               ("d_x2", [128, 16384], F32), ("d_ln2", [128, 16384], BD),
                               ("d_gt", [128, 128], BD), ("d_gfm", [8, 2048], BD),
                               ("d_glog", [8, 2048], BD), ("d_gtpre", [128, 128], BD),
                               ("d_h", [128, 4096], BD), ("d_hp", [128, 4096], BD),
                               ("d_gb", [128, 4096], F32),
                               ("d_gexp", [128, 128], F32), ("d_grec", [128, 16], F32),
                               ("d_ffn", [128, 16384], BD), ("d_rsum", [128, 512], BD),
                               ("d_bc", [128, 512], BD), ("d_prod", [128, 512], F32)]:
            dbg[nm] = nc.dram_tensor(nm, shape, dt_, kind="ExternalOutput").ap()


    ctx = ExitStack()
    arena_h = ctx.enter_context(
        nc.sbuf_tensor("arena", [128, ARENA_BYTES], mybir.dt.uint8))
    arena = arena_h[:, :]

    def reg(off_b, cols, dtype, parts=128):
        nbytes = cols * mybir.dt.size(dtype)
        assert off_b % 32 == 0 and off_b + nbytes <= ARENA_BYTES, (off_b, nbytes)
        return arena[0:parts, off_b:off_b + nbytes].bitcast(dtype)

    w1_sb = reg(0, 8192, BF16)
    w2_sb = reg(16 * KB, 8192, BF16)
    chnk = [(reg(32 * KB + p * 4096, 1024, BF16),
             reg(32 * KB + p * 4096 + 2048, 1024, BF16)) for p in (0, 1)]
    hp_sb = [reg(32 * KB + e * KB, 512, BF16) for e in range(NE)]
    ln1_fm = reg(40 * KB, 16384, BF16)
    x2_sb = reg(40 * KB, 16384, F32)
    wv_sb = reg(72 * KB, 8192, BF16)
    q_sb = reg(88 * KB, 2048, BF16)
    k_sb = reg(92 * KB, 2048, BF16)
    ptsb = [reg(96 * KB + p * 2048, 1024, BF16) for p in (0, 1, 2, 3)]
    v_sb = reg(104 * KB, 16384, BF16)
    ln2_fm = reg(104 * KB, 16384, BF16)
    attnout = reg(136 * KB, 16384, BF16)
    ffn_fm = reg(136 * KB, 16384, BF16)
    xroll = [reg(168 * KB + p * 4096, 1024, F32) for p in (0, 1)]
    oroll = [reg(176 * KB + p * 4096, 1024, F32) for p in (0, 1)]
    lnt_sb = reg(184 * KB, 1024, F32)
    ln1tm = [reg(188 * KB + p * 2048, 1024, BF16) for p in (0, 1)]
    M = 192 * KB
    ident = reg(M, 128, BF16)
    onespad = reg(M + 256, 64, BF16)
    ones1 = reg(M + 384, 128, BF16)
    bqs = reg(M + 640, NF, F32)
    bks = reg(M + 672, NF, F32)
    bvs = reg(M + 704, NF, F32)
    b1s = reg(M + 736, NE, F32, parts=HD)
    bgs = reg(M + 768, 1, F32, parts=8)
    negB = reg(M + 800, 1, F32)
    eps_sb = reg(M + 992, 1, F32)
    zero_sb = reg(M + 10 * KB + 512, 1, F32)
    gacc_sb = reg(M + 10 * KB + 544, 8, BF16)
    htmp_sb = reg(M + 11 * KB, 512, BF16)
    glog_win = reg(M + 12 * KB, 512, BF16, parts=8)
    gfm_win = reg(M + 13 * KB, 512, BF16, parts=8)
    stats_sb2 = [reg(M + 832, 12, F32), reg(M + 3200, 12, F32),
                 reg(M + 10 * KB + 576, 12, F32), reg(M + 10 * KB + 640, 12, F32),
                 reg(211968, 12, F32), reg(212032, 12, F32),
                 reg(212096, 12, F32), reg(212160, 12, F32)]
    aggr_sb2 = [reg(M + 896, 2, F32), reg(M + 3264, 2, F32),
                reg(M + 10 * KB + 704, 2, F32), reg(M + 10 * KB + 736, 2, F32),
                reg(212224, 2, F32), reg(212256, 2, F32),
                reg(212288, 2, F32), reg(212320, 2, F32)]
    std_sb2 = [reg(M + 928, 1, F32), reg(M + 3296, 1, F32),
               reg(M + 10 * KB + 768, 1, F32), reg(M + 10 * KB + 800, 1, F32),
               reg(212352, 1, F32), reg(212384, 1, F32),
               reg(212416, 1, F32), reg(212448, 1, F32)]
    rstd_sb2 = [reg(M + 960, 1, F32), reg(M + 3328, 1, F32),
                reg(M + 10 * KB + 832, 1, F32), reg(M + 10 * KB + 864, 1, F32),
                reg(212480, 1, F32), reg(212512, 1, F32),
                reg(212544, 1, F32), reg(212576, 1, F32)]
    stats_sb = stats_sb2[0]
    aggr_sb = reg(M + 896, 2, F32)
    std_sb = reg(M + 928, 1, F32)
    rstd_sb = reg(M + 960, 1, F32)
    wg_sb = reg(M + KB, 64, BF16)
    b2_sb = reg(M + KB + 128, 1024, BF16, parts=8)
    gt_sb = reg(M + 10 * KB, 128, BF16)
    gexp_sb = reg(M + 3 * KB + 512, 128, F32)
    grec_sb = reg(M + 4 * KB, 16, F32)
    grec2_sb = reg(M + 4 * KB + 64, 16, F32)
    rsum_sb = reg(M + 4 * KB + 128, 512, BF16)
    bcast_sb = reg(M + 5 * KB + 256, 512, BF16)
    prod_sb = reg(M + 6 * KB + 512, 512, F32)
    grow_sb2 = [reg(M + 9 * KB, 512, BF16, parts=1),
                reg(M + 14 * KB, 512, BF16, parts=1)]

    ps_h = ctx.enter_context(nc.psum_tensor("ps", [128, 4096], F32))
    ps = ps_h[:, :]
    psb = ps.bitcast(BF16)

    def bank(b, cols=512, dtype=F32, parts=128, p0=0):
        if dtype == F32:
            return ps[p0:p0 + parts, b * 512: b * 512 + cols]
        return psb[p0:p0 + parts, b * 1024: b * 1024 + cols]

    sems = {}
    for name in ("sWL", "sXL", "sST", "sMV", "sPE", "sVE", "sSC", "sGP", "sVB"):
        sems[name] = ctx.enter_context(nc.semaphore(name))
    sch = Sched()
    for name in sems:
        sch.sem(name)
    E = sch.emit

    def tap(name, src_ap, waits):
        if not DEBUG or (TAPS is not None and name not in TAPS):
            return 0
        def fn(sy, name=name, src_ap=src_ap):
            return sy.dma_start(dbg[name], src_ap)
        return E("sync", fn, waits=waits, inc=("sST", 16))

    ev = E("gpsimd", lambda g: g.memset(ident, 0.0), inc=("sGP", 1))
    E("gpsimd", lambda g: g.affine_select(
        out=ident, in_=ident, compare_op=OP.not_equal, fill=1.0,
        base=0, pattern=[[-1, 128]], channel_multiplier=1),
        waits=[("sGP", ev)], inc=("sGP", 1))
    ev = E("gpsimd", lambda g: g.memset(onespad, 0.0), inc=("sGP", 1))
    E("gpsimd", lambda g: g.memset(onespad[:, 0:1], 1.0),
      waits=[("sGP", ev)], inc=("sGP", 1))
    E("gpsimd", lambda g: g.memset(ones1, 1.0), inc=("sGP", 1))
    E("gpsimd", lambda g: g.memset(negB, EXP_SHIFT), inc=("sGP", 1))
    E("gpsimd", lambda g: g.memset(eps_sb, EPS), inc=("sGP", 1))
    E("gpsimd", lambda g: g.memset(zero_sb, 0.0), inc=("sGP", 1))
    gp_consts = sch.cnt["sGP"]

    def wload(fn):
        return E("gpsimd", fn, inc=("sWL", 16))

    wl_wv = wload(lambda g: g.dma_start(
        wv_sb.rearrange("p (f o) -> p f o", f=NF),
        wv_d.rearrange("(f p) o -> p f o", p=128)))
    wl_w1 = wload(lambda g: g.dma_start(
        w1_sb.rearrange("p (e f h) -> p e f h", e=NE, f=NF), w1_d))
    wl_w2 = wload(lambda g: g.dma_start(
        w2_sb.rearrange("p (e o) -> p e o", e=NE), w2_d))
    wl_wg = wload(lambda g: g.dma_start(
        wg_sb.rearrange("p (f e) -> p f e", f=NF), wg_d))
    wl_b2 = wload(lambda g: g.dma_start(b2_sb, b2_d))
    wl_bq = wload(lambda g: g.dma_start(bqs, bq_d))
    wl_bq = wload(lambda g: g.dma_start(bks, bk_d))
    wl_bv = wload(lambda g: g.dma_start(bvs, bv_d))
    wl_b1 = wload(lambda g: g.dma_start(b1s, b1_d))
    wl_bg = wload(lambda g: g.dma_start(bgs, bg_d))

    wl_chunk = [0] * NF

    def load_chunk(j, waits):
        def fq(g, j=j):
            return g.dma_start(
                chnk[j % 2][0].rearrange("p (f o) -> p f o", f=NF),
                wqj_d[j].rearrange("(f p) o -> p f o", p=128))
        E("gpsimd", fq, waits=waits, inc=("sWL", 16))

        def fk(g, j=j):
            return g.dma_start(
                chnk[j % 2][1].rearrange("p (f o) -> p f o", f=NF),
                wkj_d[j].rearrange("(f p) o -> p f o", p=128))
        wl_chunk[j] = E("gpsimd", fk, inc=("sWL", 16))

    load_chunk(0, ())
    load_chunk(1, ())

    x_tiled = x_d.rearrange("(t p) d -> t p d", p=128)
    out_tiled = out_d.rearrange("(t p) d -> t p d", p=128)
    xl_ev = {}

    def load_x(t, waits):
        def fn(sy, t=t):
            return sy.dma_start(xroll[t % 2], x_tiled[t])
        xl_ev[t] = E("sync", fn, waits=waits, inc=("sXL", 16))

    load_x(0, ())
    load_x(1, ())

    ln_state = {"sqrt": [0] * 8, "slot": 0, "cons": [0] * 8}

    def emit_ln_stats(src_ap, waits):
        slot = ln_state["slot"]
        ln_state["slot"] = (slot + 1) % 8
        st_b, ag_b = stats_sb2[slot], aggr_sb2[slot]
        sd_b, rs_b = std_sb2[slot], rstd_sb2[slot]
        for cch in range(2):
            w = (list(waits) + [("sSC", ln_state["sqrt"][slot])]) if cch == 0 else ()
            E("vector", lambda v, s=src_ap, c=cch, st=st_b: v.bn_stats(
                st[:, 6 * c:6 * c + 6], s[:, 512 * c:512 * c + 512]),
                waits=w, inc=("sVB", 1))
        vev = E("vector", lambda v, ag=ag_b, st=st_b: v.bn_aggr(ag, st),
                waits=[("sVB", sch.cnt["sVB"])], inc=("sVE", 1))
        E("scalar", lambda s, ag=ag_b, sd=sd_b: s.activation(
            sd, ag[:, 1:2], ACT.Ln, bias=eps_sb),
            waits=[("sVE", vev), ("sGP", gp_consts),
                   ("sVE", ln_state["cons"][slot])], inc=("sSC", 1))
        sev = E("scalar", lambda s, sd=sd_b, rs=rs_b: s.activation(
            rs, sd, ACT.Exp, scale=-0.5, bias=zero_sb), inc=("sSC", 1))
        ln_state["sqrt"][slot] = sev

        def consumed(ev, slot=slot):
            ln_state["cons"][slot] = ev
        return sev, ag_b, rs_b, consumed

    pe_tp1 = [0, 0]
    ve_ev1 = [0, 0]
    ve_vev = [0, 0]
    xq = [xroll[0], xroll[1], oroll[0], oroll[1]]

    def load_x1(t, waits):
        def fn(sy, t=t):
            return sy.dma_start(xq[t % 4], x_tiled[t])
        xl_ev[t] = E("sync", fn, waits=waits, inc=("sXL", 16))
    load_x1(2, ())
    load_x1(3, ())
    for t in range(NT):
        par = t % 2
        rstd_ev, ag_a, rs_a, cons = emit_ln_stats(xq[t % 4], [("sXL", xl_ev[t])])
        vap = E("vector", lambda v, t=t, ag=ag_a, rs=rs_a: v.tensor_scalar(
            ln1tm[t % 2], xq[t % 4], ag[:, 0:1], rs,
            op0=OP.subtract, op1=OP.mult),
            waits=[("sSC", rstd_ev), ("sPE", pe_tp1[par])], inc=("sVE", 1))
        cons(vap)
        if t + 4 < NT:
            load_x1(t + 4, [("sVE", vap)])
        for fc in range(NF):
            w = [("sVE", vap), ("sVE", ve_ev1[par]), ("sGP", gp_consts)] \
                if fc == 0 else ()
            pe_tp1[par] = E("tensor", lambda tt_, fc=fc, par=par: tt_.transpose(
                bank(par, 1024, BF16)[:, 128 * fc:128 * fc + 128],
                ln1tm[par][:, 128 * fc:128 * fc + 128], ident),
                waits=w, inc=("sPE", 1))
        dst = ln1_fm.rearrange("p (f s) -> p f s", f=NF)[:, :, 128 * t:128 * t + 128]
        ve_ev1[par] = E("vector", lambda v, dst=dst, par=par: v.tensor_copy(
            dst, bank(par, 1024, BF16).rearrange("p (f s) -> p f s", f=NF)),
            waits=[("sPE", pe_tp1[par])], inc=("sVE", 1))
        for oh in range(2):
            b = 2 + oh
            w0 = [("sVE", ve_ev1[par]), ("sWL", wl_wv), ("sVE", ve_vev[oh])]
            for fc in range(NF):
                stat = ln1_fm[:, fc * 2048 + 128 * t: fc * 2048 + 128 * t + 128]
                mov = wv_sb[:, fc * 1024 + 512 * oh: fc * 1024 + 512 * oh + 512]
                pe_v = E("tensor", lambda t_, s=stat, m=mov, b=b, fc=fc: t_.matmul(
                    bank(b), s, m, start=(fc == 0), stop=(fc == NF - 1)),
                    waits=w0 if fc == 0 else (), inc=("sPE", 1))
            dst = v_sb[:, t * 1024 + 512 * oh: t * 1024 + 512 * oh + 512]
            ve_vev[oh] = E("vector", lambda v, d=dst, b=b: v.tensor_copy(
                d, bank(b)), waits=[("sPE", pe_v)], inc=("sVE", 1))
    ln1_done = max(ve_ev1)
    v_done = max(ve_vev)

    att_exp_last = 0
    qkv_bank_ev = [ve_ev1[0], ve_ev1[1], ve_vev[0], ve_vev[1]]
    exp_hist = {}
    ve_pack_hist = {}
    ve_bc_hist = {}
    ve_prod_hist = {}
    pend = {"a": None, "b": None}
    gslot = 0
    att_done = 0

    def emit_pv(j, c, w, kc):
        p4 = kc % 4
        ob = 4 + (w % 2)
        sb_bank = 6 + (w % 2)
        for h in (0, 1):
            vstat = v_sb[:, kc * 1024 + 128 * j + 64 * h:
                         kc * 1024 + 128 * j + 64 * h + 64]
            mov = ptsb[p4][:, 512 * h:512 * h + 512]
            wts = [("sSC", exp_hist[j * 64 + c * 16 + kc])]
            if kc == 0 and w >= 2 and (w - 2) in ve_pack_hist:
                wts.append(("sVE", ve_pack_hist[w - 2]))
            E("tensor", lambda t_, vs=vstat, m=mov, h=h, kc=kc, ob=ob:
              t_.matmul(bank(ob, 512, F32, 64, 64 * h), vs, m,
                        start=(kc == 0), stop=(kc == NT - 1),
                        skip_group_check=True),
              waits=wts if h == 0 else (), inc=("sPE", 1))
        for h in (0, 1):
            mov = ptsb[p4][:, 512 * h:512 * h + 512]
            wts2 = []
            if h == 0 and kc == 0 and w >= 2 and (w - 2) in ve_bc_hist:
                wts2.append(("sVE", ve_bc_hist[w - 2]))
            E("tensor", lambda t_, m=mov, h=h, kc=kc, b=sb_bank:
              t_.matmul(bank(b, 512, F32, 64, 64 * h),
                        onespad[:, 0:64], m,
                        start=(kc == 0), stop=(kc == NT - 1),
                        skip_group_check=True),
              waits=wts2, inc=("sPE", 1))

    pe_rank_hist = {}

    def emit_norm_a(j, w, ve_recip):
        par = w % 2
        sb_b = 6 + par
        ob_b = 4 + par
        pe_b = E("tensor", lambda t_, b=sb_b: t_.matmul(
            bank(b, 512, F32, 64, 0), ones1[0:1, 0:64], rsum_sb[0:1, :],
            start=True, stop=True, skip_group_check=True),
            waits=[("sSC", ve_recip)], inc=("sPE", 1))
        pe_b = E("tensor", lambda t_, b=sb_b: t_.matmul(
            bank(b, 512, F32, 64, 64), ones1[64:65, 0:64], rsum_sb[64:65, :],
            start=True, stop=True, skip_group_check=True), inc=("sPE", 1))
        pe_rank_hist[w] = pe_b
        ve_bc = E("vector", lambda v, b=sb_b: v.tensor_copy(bcast_sb, bank(b)),
                  waits=[("sPE", pe_b)], inc=("sVE", 1))
        ve_bc_hist[w] = ve_bc
        vev = E("vector", lambda v, b=ob_b: v.scalar_tensor_tensor(
            prod_sb, bank(b), 1.0, bcast_sb, op0=OP.mult, op1=OP.mult),
            waits=[("sVE", ve_bc)], inc=("sVE", 1))
        ve_prod = E("vector", lambda v, j=j: v.tensor_scalar(
            bcast_sb, prod_sb, bvs[:, j:j + 1], None, op0=OP.add),
            waits=[("sVE", vev), ("sWL", wl_bv)], inc=("sVE", 1))
        ve_prod_hist[w] = ve_prod

    def emit_norm_b(j, c, w):
        par = w % 2
        ob_b = 4 + par
        pe_tp = 0
        for qt in range(4):
            wts = [("sVE", ve_prod_hist[w])] if qt == 0 else ()
            pe_tp = E("tensor", lambda t_, qt=qt, b=ob_b: t_.transpose(
                bank(b, 1024, BF16)[:, 128 * qt:128 * qt + 128],
                bcast_sb[:, 128 * qt:128 * qt + 128], ident),
                waits=wts, inc=("sPE", 1))
        dst = attnout.rearrange("p (t f) -> p t f", t=NT)[
            :, 4 * c:4 * c + 4, 128 * j:128 * j + 128]
        ve_pack_hist[w] = E("vector", lambda v, d=dst, b=ob_b: v.tensor_copy(
            d, bank(b, 512, BF16).rearrange("p (t f) -> p t f", t=4)),
            waits=[("sPE", pe_tp)], inc=("sVE", 1))

    for j in range(NF):
        for (pidx, bias_ap, dest) in ((0, bqs, q_sb), (1, bks, k_sb)):
            for c in range(4):
                b = (pidx * 4 + c) % 4
                w0 = [("sWL", wl_chunk[j]), ("sVE", ln1_done),
                      ("sVE", qkv_bank_ev[b]), ("sSC", att_exp_last)]
                for fc in range(NF):
                    stat = chnk[j % 2][pidx][:, fc * 128: fc * 128 + 128]
                    mov = ln1_fm[:, fc * 2048 + 512 * c: fc * 2048 + 512 * c + 512]
                    pe_ev = E("tensor", lambda t_, s=stat, m=mov, b=b, fc=fc:
                              t_.matmul(bank(b), s, m,
                                        start=(fc == 0), stop=(fc == NF - 1)),
                              waits=w0 if fc == 0 else (), inc=("sPE", 1))
                dst = dest[:, 512 * c: 512 * c + 512]
                qkv_bank_ev[b] = E("vector", lambda v, d=dst, b=b, ba=bias_ap, j=j:
                                   v.tensor_scalar(d, bank(b), ba[:, j:j + 1], None,
                                                   op0=OP.add),
                                   waits=[("sPE", pe_ev), ("sWL", wl_bq)],
                                   inc=("sVE", 1))
            if pidx == 0 and pend["a"] is not None:
                pend["a"]()
                pend["a"] = None
            if pidx == 1 and pend["b"] is not None:
                pend["b"]()
                pend["b"] = None
        qk_done = max(qkv_bank_ev)
        if j + 2 < NF:
            load_chunk(j + 2, [("sPE", pe_ev)])

        for c in range(4):
            w = j * 4 + c
            par_w = w % 2
            ob = 4 + par_w
            sb_bank = 6 + par_w
            q0 = 512 * c
            prev_slot = {}
            for kc in range(NT):
                par = kc % 2
                pb = 2 * par
                g = gslot
                gslot += 1
                wts = [("sVE", qk_done)]
                if g - 2 in exp_hist:
                    wts.append(("sSC", exp_hist[g - 2]))
                if j == 0 and c == 0 and kc < 2:
                    wts.append(("sVE", v_done))
                for h in (0, 1):
                    r0 = 64 * h
                    stat = k_sb[r0:r0 + 64, 128 * kc:128 * kc + 128]
                    mov = q_sb[r0:r0 + 64, q0:q0 + 512]
                    pe_st = E("tensor", lambda t_, s=stat, m=mov, pb=pb, h=h:
                              t_.matmul(bank(pb + h), s, m, start=True, stop=True),
                              waits=wts if h == 0 else (), inc=("sPE", 1))
                if kc == 4 and pend["a"] is not None:
                    pend["a"]()
                    pend["a"] = None
                if kc == 12 and pend["b"] is not None:
                    pend["b"]()
                    pend["b"] = None
                if kc > 1:
                    emit_pv(j, c, w, kc - 2)
                sc_ev = E("scalar", lambda s_, pb=pb, p4=kc % 4: s_.activation(
                    ptsb[p4], bank(pb, 1024), ACT.Exp, bias=negB, scale=0.125),
                    waits=[("sPE", pe_st), ("sGP", gp_consts)], inc=("sSC", 1))
                exp_hist[g] = sc_ev
                att_exp_last = sc_ev
            emit_pv(j, c, w, NT - 2)
            emit_pv(j, c, w, NT - 1)
            pe_sums_last = sch.cnt["sPE"]

            wr = [("sPE", pe_sums_last)]
            if (w - 1) in pe_rank_hist:
                wr.append(("sPE", pe_rank_hist[w - 1]))
            if (w - 1) in ve_prod_hist:
                wr.append(("sVE", ve_prod_hist[w - 1]))

            def f_ln_a(s, b=sb_bank):
                return s.activation(prod_sb[0:1, :], bank(b, 512, F32, 1, 0),
                                    ACT.Ln, bias=zero_sb[0:1, :])

            def f_ex_a(s):
                with nc.allow_low_precision(reason="rsum feeds bf16 matmul"):
                    return s.activation(rsum_sb[0:1, :], prod_sb[0:1, :],
                                        ACT.Exp, scale=-1.0, bias=zero_sb[0:1, :])

            def f_ln_b(s, b=sb_bank):
                return s.activation(prod_sb[64:65, :], bank(b, 512, F32, 1, 64),
                                    ACT.Ln, bias=zero_sb[64:65, :])

            def f_ex_b(s):
                with nc.allow_low_precision(reason="rsum feeds bf16 matmul"):
                    return s.activation(rsum_sb[64:65, :], prod_sb[64:65, :],
                                        ACT.Exp, scale=-1.0, bias=zero_sb[64:65, :])
            E("scalar", f_ln_a, waits=wr, inc=("sSC", 1))
            E("scalar", f_ex_a, inc=("sSC", 1))
            E("scalar", f_ln_b, inc=("sSC", 1))
            ve_recip = E("scalar", f_ex_b, inc=("sSC", 1))

            def mk_pend(j=j, c=c, w=w, ve_recip=ve_recip):
                def fa():
                    emit_norm_a(j, w, ve_recip)

                def fb():
                    emit_norm_b(j, c, w)
                return fa, fb
            pend["a"], pend["b"] = mk_pend()
    pend["a"]()
    pend["b"]()
    att_done = ve_pack_hist[NF * 4 - 1]

    tap("d_att", attnout, [("sVE", att_done)])

    tp_ev = [0, 0]
    tp_pe = [0, 0]
    tp_next = [0]

    def load_x4(t, waits):
        def fn(sy, t=t):
            return sy.dma_start(xroll[t % 2], x_tiled[t])
        xl_ev[t] = E("sync", fn, waits=waits, inc=("sXL", 16))

    att_exp_final = att_exp_last
    load_x4(0, [("sVE", ve_pack_hist[28])])
    load_x4(1, [("sVE", ve_pack_hist[28])])

    ve_ln2win = [0, 0, 0, 0]
    ve_glog = 0
    ve_gfm_prev = 0
    gfm_evs = {}
    pe_rank = 0
    pe_rank_hist2 = {}
    pe_down_last = 0
    ve_hp = {}
    ve_dev = [0, 0]
    st_ev = [0] * NT
    pair_read = [0, 0]
    vadd_ev = {}
    ln_b_all = {}

    def preABC(c):
        t0 = 4 * c
        ln_a = {}
        for q in range(4):
            a_t = attnout[:, 1024 * (t0 + q):1024 * (t0 + q) + 1024]
            ln_a[q] = emit_ln_stats(a_t, [("sVE", ve_pack_hist[28 + c])])
        for q in range(4):
            t = t0 + q
            a_t = attnout[:, 1024 * t:1024 * t + 1024]
            rstd_ev, ag_a, rs_a, cons = ln_a[q]
            vev = E("vector", lambda v, a=a_t, ag=ag_a, rs=rs_a:
                    v.tensor_scalar(
                lnt_sb, a, ag[:, 0:1], rs, op0=OP.subtract, op1=OP.mult),
                waits=[("sSC", rstd_ev)], inc=("sVE", 1))
            cons(vev)
            x2_t = x2_sb[:, 1024 * t:1024 * t + 1024]
            vadd = E("vector", lambda v, x2=x2_t, xp=t % 2: v.tensor_add(
                x2, xroll[xp], lnt_sb),
                waits=[("sVE", vev), ("sXL", xl_ev[t])], inc=("sVE", 1))
            if t + 2 < NT:
                load_x4(t + 2, [("sVE", vadd)])
            vadd_ev[t] = vadd
        ln_b = {}
        for q in range(4):
            t = t0 + q
            ln_b[q] = emit_ln_stats(
                x2_sb[:, 1024 * t:1024 * t + 1024], [("sVE", vadd_ev[t])])
        ln_b_all[c] = ln_b

    preABC(0)

    def emit_D_pair(c, pair):
        ln_b = ln_b_all[c]
        t0 = 4 * c
        par_of = {}

        def p4_d_ap(q):
            t = t0 + q
            par_of[q] = tp_next[0] % 2
            tp_next[0] += 1
            rstd_ev, ag_a, rs_a, cons = ln_b[q]
            x2_t = x2_sb[:, 1024 * t:1024 * t + 1024]
            vap = E("vector", lambda v, x2=x2_t, lp=t % 2, ag=ag_a, rs=rs_a:
                    v.tensor_scalar(
                ln1tm[lp], x2, ag[:, 0:1], rs,
                op0=OP.subtract, op1=OP.mult),
                waits=[("sSC", rstd_ev), ("sPE", tp_pe[par_of[q]])],
                inc=("sVE", 1))
            cons(vap)
            return vap

        def p4_d_tp(q, vap):
            t = t0 + q
            par = par_of[q]
            for fc in range(NF):
                w = [("sVE", vap), ("sVE", tp_ev[par]),
                     ("sSC", att_exp_final)] if fc == 0 else ()
                tp_pe[par] = E("tensor", lambda t_, fc=fc, par=par, lp=t % 2:
                               t_.transpose(
                    bank(par, 1024, BF16)[:, 128 * fc:128 * fc + 128],
                    ln1tm[lp][:, 128 * fc:128 * fc + 128], ident),
                    waits=w, inc=("sPE", 1))
            dst = ln2_fm.rearrange("p (f s) -> p f s", f=NF)[
                :, :, 128 * t:128 * t + 128]
            tp_ev[par] = E("vector", lambda v, dst=dst, par=par: v.tensor_copy(
                dst, bank(par, 1024, BF16).rearrange("p (f s) -> p f s", f=NF)),
                waits=[("sPE", tp_pe[par])], inc=("sVE", 1))
            ve_ln2win[c] = max(ve_ln2win[c], tp_ev[par])

        vaps = [p4_d_ap(2 * pair), p4_d_ap(2 * pair + 1)]
        p4_d_tp(2 * pair, vaps[0])
        p4_d_tp(2 * pair + 1, vaps[1])

    emit_D_pair(0, 0)
    for c in range(4):
        emit_D_pair(c, 1)

        w0g = [("sVE", ve_ln2win[c]), ("sWL", wl_wg), ("sVE", ve_gfm_prev)]
        if c == 0:
            w0g.append(("sVE", ve_pack_hist[30]))
        for fc in range(NF):
            stat = wg_sb[:, fc * 8: fc * 8 + 8]
            mov = ln2_fm[:, fc * 2048 + 512 * c: fc * 2048 + 512 * c + 512]
            pe_g = E("tensor", lambda t_, s=stat, m=mov, fc=fc: t_.matmul(
                bank(4, 512, F32, 8, 0), s, m, start=(fc == 0), stop=(fc == NF - 1)),
                waits=w0g if fc == 0 else (), inc=("sPE", 1))
        ve_glog = E("vector", lambda v: v.tensor_scalar(
            glog_win, bank(4, 512, F32, 8, 0),
            bgs, None, op0=OP.add),
            waits=[("sPE", pe_g), ("sWL", wl_bg)], inc=("sVE", 1))
        pe_h_pre = {}
        for e in (0, 1):
            hb = 2 + e % 2
            prior = ve_hp.get((e - 2) % NE, 0) if c > 0 else 0
            w0 = [("sVE", ve_ln2win[c]), ("sWL", wl_w1), ("sVE", prior),
                  ("sSC", att_exp_final)]
            if c > 0:
                w0.append(("sVE", pair_read[1]))
            for fc in range(NF):
                stat = w1_sb[:, (e * 8 + fc) * 128:(e * 8 + fc) * 128 + 128]
                mov = ln2_fm[:, fc * 2048 + 512 * c: fc * 2048 + 512 * c + 512]
                pe_h_pre[e] = E("tensor", lambda t_, s=stat, m=mov, hb=hb, fc=fc:
                                t_.matmul(
                    bank(hb), s, m, start=(fc == 0), stop=(fc == NF - 1)),
                    waits=w0 if fc == 0 else (), inc=("sPE", 1))
        pe_gt = 0
        for q in range(4):
            cc = 4 * c + q
            w = [("sVE", ve_glog)] if q == 0 else ()
            pe_gt = E("tensor", lambda t_, q=q: t_.transpose(
                bank(4, 1024, BF16)[:, 512 + 8 * q:512 + 8 * q + 8],
                glog_win[:, 128 * q:128 * q + 128], ident[0:8, 0:8]),
                waits=w, inc=("sPE", 1))
        ve_gt = E("vector", lambda v, c=c: v.tensor_copy(
            gt_sb[:, 32 * c:32 * c + 32], bank(4, 1024, BF16)[:, 512:512 + 32]),
            waits=[("sPE", pe_gt)], inc=("sVE", 1))
        sc_g = E("scalar", lambda s, c=c: s.activation(
            gexp_sb[:, 32 * c:32 * c + 32], gt_sb[:, 32 * c:32 * c + 32],
            ACT.Exp, bias=zero_sb),
            waits=[("sVE", ve_gt)], inc=("sSC", 1))
        ve_gtm = 0
        for q in range(4):
            tt = 4 * c + q
            vev = E("vector", lambda v, tt=tt: v.tensor_scalar(
                gacc_sb, gexp_sb[:, 8 * tt:8 * tt + 8], 1.0, None,
                op0=OP.mult, op1=OP.add, accum_out=grec_sb[:, tt:tt + 1]),
                waits=[("sSC", sc_g)], inc=("sVE", 1))
            E("scalar", lambda s, tt=tt: s.activation(
                std_sb, grec_sb[:, tt:tt + 1], ACT.Ln, bias=zero_sb),
                waits=[("sVE", vev)], inc=("sSC", 1))
            scv = E("scalar", lambda s, tt=tt: s.activation(
                grec2_sb[:, tt:tt + 1], std_sb, ACT.Exp, scale=-1.0, bias=zero_sb),
                inc=("sSC", 1))
            ve_gtm = E("vector", lambda v, tt=tt: v.tensor_scalar(
                gt_sb[:, 8 * tt:8 * tt + 8], gexp_sb[:, 8 * tt:8 * tt + 8],
                grec2_sb[:, tt:tt + 1], None, op0=OP.mult),
                waits=[("sSC", scv)], inc=("sVE", 1))
        pe_gf = 0
        for q in range(4):
            tt = 4 * c + q
            w = [("sVE", ve_gtm)] if q == 0 else ()
            pe_gf = E("tensor", lambda t_, tt=tt, q=q: t_.transpose(
                bank(4, 1024, BF16, 8, 0)[:, 128 * q:128 * q + 128],
                gt_sb[:, 8 * tt:8 * tt + 8], ident),
                waits=w, inc=("sPE", 1))
        wgf = [("sPE", pe_gf)]
        if c > 0:
            wgf.append(("sMV", mv_grow_last))
        ve_gfm_prev = E("vector", lambda v: v.tensor_copy(
            gfm_win, bank(4, 1024, BF16, 8, 0)[:, 0:512]),
            waits=wgf, inc=("sVE", 1))
        gfm_evs[c] = ve_gfm_prev

        for e in range(NE):
            mv_grow = E("sync", lambda sy, e=e: sy.dma_start(
                grow_sb2[e % 2], gfm_win[e:e + 1, :]),
                waits=[("sVE", gfm_evs[c]), ("sPE", pe_rank_hist2.get((e - 2) % NE, 0))],
                inc=("sMV", 16))
            mv_grow_last = mv_grow
            hb = 2 + e % 2
            prior = ve_hp.get((e - 2) % NE, 0) if (e >= 2 or c > 0) else 0
            w0 = [("sVE", ve_ln2win[c]), ("sWL", wl_w1), ("sVE", prior),
                  ("sSC", att_exp_final)]
            if e < 2 and c > 0:
                w0.append(("sVE", pair_read[1]))
            for fc in range(NF):
                stat = w1_sb[:, (e * 8 + fc) * 128:(e * 8 + fc) * 128 + 128]
                mov = ln2_fm[:, fc * 2048 + 512 * c: fc * 2048 + 512 * c + 512]
                pe_h = E("tensor", lambda t_, s=stat, m=mov, hb=hb, fc=fc: t_.matmul(
                    bank(hb), s, m, start=(fc == 0), stop=(fc == NF - 1)),
                    waits=w0 if fc == 0 else (), inc=("sPE", 1))
            wr1 = [("sMV", mv_grow), ("sVE", prior)]
            if c == 0 and e == 0:
                wr1.append(("sVE", ve_pack_hist[31]))
            pe_rank = E("tensor", lambda t_, e=e: t_.matmul(
                bank(5, 512, F32, 128, 0), ones1[0:1, :], grow_sb2[e % 2],
                start=True, stop=True), waits=wr1, inc=("sPE", 1))
            pe_rank_hist2[e] = pe_rank
            vev = E("vector", lambda v, hb=hb, e=e: v.tensor_scalar(
                htmp_sb, bank(hb), b1s[:, e:e + 1], 0.0, op0=OP.add, op1=OP.max),
                waits=[("sPE", pe_h), ("sWL", wl_b1), ("sPE", pe_down_last)],
                inc=("sVE", 1))
            ve_hp[e] = E("vector", lambda v, e=e: v.scalar_tensor_tensor(
                hp_sb[e], htmp_sb, 1.0, bank(5, 512), op0=OP.mult, op1=OP.mult),
                waits=[("sVE", vev), ("sPE", pe_rank)], inc=("sVE", 1))
        down_done = {}

        def emit_down(tc):
            b0 = (6, 2)[tc % 2]
            for half in range(2):
                for e in range(NE):
                    wts = []
                    if half == 0:
                        if e == 0:
                            wts = [("sWL", wl_w2), ("sVE", ve_hp[0])]
                            if tc % 2 == 0:
                                if c == 0 and tc == 0:
                                    wts += [("sVE", ve_bc_hist[30]),
                                            ("sVE", ve_bc_hist[31])]
                                else:
                                    wts.append(("sVE", pair_read[0]))
                            else:
                                wts += [("sVE", ve_hp[6]), ("sVE", ve_hp[7]),
                                        ("sVE", pair_read[1])]
                        else:
                            wts = [("sVE", ve_hp[e])]
                    stat = hp_sb[e][:, 128 * tc:128 * tc + 128]
                    mov = w2_sb[:, e * 1024 + 512 * half:
                                e * 1024 + 512 * half + 512]
                    E("tensor", lambda t_, s=stat, m=mov, b=b0 + half, e=e:
                      t_.matmul(bank(b), s, m, start=(e == 0), stop=False),
                      waits=wts, inc=("sPE", 1))
                last = E("tensor", lambda t_, tc=tc, b=b0 + half, half=half:
                         t_.matmul(
                    bank(b), gfm_win[:, 128 * tc:128 * tc + 128],
                    b2_sb[:, 512 * half:512 * half + 512],
                    start=False, stop=True),
                    waits=[("sWL", wl_b2), ("sVE", gfm_evs[c])], inc=("sPE", 1))
            down_done[tc] = last

        def emit_p6(tc):
            t = 4 * c + tc
            view = bank((6, 2)[tc % 2], 1024, F32)
            rstd_ev, ag_a, rs_a, cons = emit_ln_stats(
                view, [("sPE", down_done[tc])])
            vev = E("vector", lambda v, view=view, ag=ag_a, rs=rs_a:
                    v.tensor_scalar(
                lnt_sb, view, ag[:, 0:1], rs, op0=OP.subtract, op1=OP.mult),
                waits=[("sSC", rstd_ev)], inc=("sVE", 1))
            cons(vev)
            pair_read[tc % 2] = vev
            x2_t = x2_sb[:, 1024 * t:1024 * t + 1024]
            w = [("sVE", vev)]
            if t >= 2:
                w.append(("sST", st_ev[t - 2]))
            vout = E("vector", lambda v, x2=x2_t, op=t % 2: v.tensor_add(
                oroll[op], x2, lnt_sb), waits=w, inc=("sVE", 1))
            st_ev[t] = E("sync", lambda sy, t=t, op=t % 2: sy.dma_start(
                out_tiled[t], oroll[op]), waits=[("sVE", vout)], inc=("sST", 16))

        emit_down(0)
        emit_down(1)
        emit_p6(0)
        emit_down(2)
        emit_p6(1)
        emit_down(3)
        pe_down_last = down_done[3]
        if c + 1 < 4:
            preABC(c + 1)
            emit_D_pair(c + 1, 0)
        emit_p6(2)
        emit_p6(3)
    final_st = sch.cnt["sST"]
    E("sync", lambda sy: sy.wait_ge(sems["sST"], final_st), waits=())

    block = ctx.enter_context(nc.Block())

    def mk(name):
        def body(eng):
            run_engine(eng, name, sch.ops[name], sems)
        return body

    block.sync(mk("sync"))
    block.tensor(mk("tensor"))
    block.vector(mk("vector"))
    block.scalar(mk("scalar"))
    block.gpsimd(mk("gpsimd"))
    ctx.close()
    return nc


def host_layouts(inputs):
    f32 = np.float32
    Wq = np.asarray(inputs["L0_attn_Wq"], f32)
    Wk = np.asarray(inputs["L0_attn_Wk"], f32)
    Wv = np.asarray(inputs["L0_attn_Wv"], f32)
    W1 = np.asarray(inputs["L0_ffn_W1"], f32)
    W2 = np.asarray(inputs["L0_ffn_W2"], f32)
    Wg = np.asarray(inputs["L0_ffn_Wg"], f32)
    d = {}
    wqT = np.ascontiguousarray(Wq.T)
    wkT = np.ascontiguousarray(Wk.T)
    d["wqj"] = np.ascontiguousarray(wqT.reshape(D, NF, 128).transpose(1, 0, 2))
    d["wkj"] = np.ascontiguousarray(wkT.reshape(D, NF, 128).transpose(1, 0, 2))
    d["wv"] = np.ascontiguousarray(Wv.T)
    d["w1l"] = np.ascontiguousarray(W1.reshape(NE, NF, 128, HD).transpose(2, 0, 1, 3))
    d["w2l"] = np.ascontiguousarray(W2.transpose(1, 0, 2))
    d["wgl"] = np.ascontiguousarray(Wg.reshape(NF, 128, NE).transpose(1, 0, 2))
    d["bq2"] = np.ascontiguousarray(np.asarray(inputs["L0_attn_bq"], f32).reshape(NF, 128).T)
    d["bk2"] = np.ascontiguousarray(np.asarray(inputs["L0_attn_bk"], f32).reshape(NF, 128).T)
    d["bv2"] = np.ascontiguousarray(np.asarray(inputs["L0_attn_bv"], f32).reshape(NF, 128).T)
    d["b12"] = np.ascontiguousarray(np.asarray(inputs["L0_ffn_b1"], f32).T)
    d["b2l"] = np.ascontiguousarray(np.asarray(inputs["L0_ffn_b2"], f32))
    d["bg2"] = np.ascontiguousarray(np.asarray(inputs["L0_ffn_bg"], f32).reshape(NE, 1))
    return d


_NC_CACHE = None


def kernel(**inputs):
    global _NC_CACHE
    if _NC_CACHE is None:
        _NC_CACHE = build_nc()
    nc = _NC_CACHE
    shared = host_layouts(inputs)
    x = np.asarray(inputs["L0_attn_x"], np.float32)
    in_maps = []
    for i in range(8):
        m = dict(shared)
        m["x"] = np.ascontiguousarray(x[i])
        in_maps.append(m)
    res = run_bass_kernel_spmd(nc, in_maps, core_ids=list(range(8)))
    if DEBUG:
        kernel.debug = res.results
    return np.stack([np.asarray(r["out"], np.float32) for r in res.results], axis=0)


# version 26
# speedup vs baseline: 1.0328x; 1.0328x over previous
import numpy as np
import concourse.bass as bass
import concourse.mybir as mybir
from concourse.bass_utils import run_bass_kernel_spmd
from contextlib import ExitStack

F32 = mybir.dt.float32
BF16 = mybir.dt.bfloat16
AX = mybir.AxisListType
OP = mybir.AluOpType
ACT = mybir.ActivationFunctionType

S, D, NT, NF, NE, HD = 2048, 1024, 16, 8, 8, 128
DEBUG = False
TAPS = None
EXP_SHIFT = -16.0
EPS = 1e-5
KB = 1024
ARENA_BYTES = 212832
ENGINES = ("sync", "tensor", "vector", "scalar", "gpsimd")
OWN_SEM = {"tensor": "sPE", "vector": "sVE", "scalar": "sSC"}


class Sched:
    def __init__(self):
        self.ops = {e: [] for e in ENGINES}
        self.cnt = {}

    def sem(self, name):
        self.cnt[name] = 0

    def emit(self, eng, fn, waits=(), inc=None):
        waits = [(s, v) for (s, v) in waits if v > 0]
        for s, v in waits:
            assert v <= self.cnt[s], f"wait on future event {s}={v} (now {self.cnt[s]})"
        if inc is not None:
            self.cnt[inc[0]] += inc[1]
        self.ops[eng].append((tuple(waits), fn, inc))
        return self.cnt[inc[0]] if inc is not None else None


def run_engine(eng_obj, eng_name, ops, sems):
    own = OWN_SEM.get(eng_name)
    watermark = {}
    for waits, fn, inc in ops:
        for s, v in waits:
            if s == own or watermark.get(s, 0) >= v:
                continue
            eng_obj.wait_ge(sems[s], v)
            watermark[s] = v
        ins = fn(eng_obj)
        if inc is not None and ins is not None:
            ins.then_inc(sems[inc[0]], inc[1])


def build_nc():
    nc = bass.Bass("TRN2", target_bir_lowering=False, debug=False,
                   detect_race_conditions=False)

    x_d = nc.dram_tensor("x", [S, D], F32, kind="ExternalInput").ap()
    wqj_d = nc.dram_tensor("wqj", [NF, D, 128], F32, kind="ExternalInput").ap()
    wkj_d = nc.dram_tensor("wkj", [NF, D, 128], F32, kind="ExternalInput").ap()
    wv_d = nc.dram_tensor("wv", [D, D], F32, kind="ExternalInput").ap()
    w1_d = nc.dram_tensor("w1l", [128, NE, NF, HD], F32, kind="ExternalInput").ap()
    w2_d = nc.dram_tensor("w2l", [HD, NE, D], F32, kind="ExternalInput").ap()
    wg_d = nc.dram_tensor("wgl", [128, NF, NE], F32, kind="ExternalInput").ap()
    bq_d = nc.dram_tensor("bq2", [128, NF], F32, kind="ExternalInput").ap()
    bk_d = nc.dram_tensor("bk2", [128, NF], F32, kind="ExternalInput").ap()
    bv_d = nc.dram_tensor("bv2", [128, NF], F32, kind="ExternalInput").ap()
    b1_d = nc.dram_tensor("b12", [HD, NE], F32, kind="ExternalInput").ap()
    b2_d = nc.dram_tensor("b2l", [NE, D], F32, kind="ExternalInput").ap()
    bg_d = nc.dram_tensor("bg2", [NE, 1], F32, kind="ExternalInput").ap()
    out_d = nc.dram_tensor("out", [S, D], F32, kind="ExternalOutput").ap()
    dbg = {}
    if DEBUG:
        BD = mybir.dt.bfloat16
        for nm, shape, dt_ in [("d_ln1fm", [128, 16384], BD), ("d_q", [128, 2048], BD),
                               ("d_k", [128, 2048], BD), ("d_v", [128, 16384], BD),
                               ("d_pt0", [128, 1024], BD), ("d_att", [128, 16384], BD),
                               ("d_x2", [128, 16384], F32), ("d_ln2", [128, 16384], BD),
                               ("d_gt", [128, 128], BD), ("d_gfm", [8, 2048], BD),
                               ("d_glog", [8, 2048], BD), ("d_gtpre", [128, 128], BD),
                               ("d_h", [128, 4096], BD), ("d_hp", [128, 4096], BD),
                               ("d_gb", [128, 4096], F32),
                               ("d_gexp", [128, 128], F32), ("d_grec", [128, 16], F32),
                               ("d_ffn", [128, 16384], BD), ("d_rsum", [128, 512], BD),
                               ("d_bc", [128, 512], BD), ("d_prod", [128, 512], F32)]:
            dbg[nm] = nc.dram_tensor(nm, shape, dt_, kind="ExternalOutput").ap()


    ctx = ExitStack()
    arena_h = ctx.enter_context(
        nc.sbuf_tensor("arena", [128, ARENA_BYTES], mybir.dt.uint8))
    arena = arena_h[:, :]

    def reg(off_b, cols, dtype, parts=128):
        nbytes = cols * mybir.dt.size(dtype)
        assert off_b % 32 == 0 and off_b + nbytes <= ARENA_BYTES, (off_b, nbytes)
        return arena[0:parts, off_b:off_b + nbytes].bitcast(dtype)

    w1_sb = reg(0, 8192, BF16)
    w2_sb = reg(16 * KB, 8192, BF16)
    chnk = [(reg(32 * KB + p * 4096, 1024, BF16),
             reg(32 * KB + p * 4096 + 2048, 1024, BF16)) for p in (0, 1)]
    hp_sb = [reg(32 * KB + e * KB, 512, BF16) for e in range(NE)]
    ln1_fm = reg(40 * KB, 16384, BF16)
    x2_sb = reg(40 * KB, 16384, F32)
    wv_sb = reg(72 * KB, 8192, BF16)
    q_sb = reg(88 * KB, 2048, BF16)
    k_sb = reg(92 * KB, 2048, BF16)
    ptsb = [reg(96 * KB + p * 2048, 1024, BF16) for p in (0, 1, 2, 3)]
    v_sb = reg(104 * KB, 16384, BF16)
    ln2_fm = reg(104 * KB, 16384, BF16)
    attnout = reg(136 * KB, 16384, BF16)
    ffn_fm = reg(136 * KB, 16384, BF16)
    xroll = [reg(168 * KB + p * 4096, 1024, F32) for p in (0, 1)]
    oroll = [reg(176 * KB + p * 4096, 1024, F32) for p in (0, 1)]
    lnt_sb = reg(184 * KB, 1024, F32)
    ln1tm = [reg(188 * KB + p * 2048, 1024, BF16) for p in (0, 1)]
    M = 192 * KB
    ident = reg(M, 128, BF16)
    onespad = reg(M + 256, 64, BF16)
    ones1 = reg(M + 384, 128, BF16)
    bqs = reg(M + 640, NF, F32)
    bks = reg(M + 672, NF, F32)
    bvs = reg(M + 704, NF, F32)
    b1s = reg(M + 736, NE, F32, parts=HD)
    bgs = reg(M + 768, 1, F32, parts=8)
    negB = reg(M + 800, 1, F32)
    eps_sb = reg(M + 992, 1, F32)
    zero_sb = reg(M + 10 * KB + 512, 1, F32)
    gacc_sb = reg(M + 10 * KB + 544, 8, BF16)
    htmp_sb2 = [reg(M + 11 * KB, 512, BF16), reg(136 * KB, 512, BF16)]
    glog_win = reg(M + 12 * KB, 512, BF16, parts=8)
    gfm_win = reg(M + 13 * KB, 512, BF16, parts=8)
    stats_sb2 = [reg(M + 832, 12, F32), reg(M + 3200, 12, F32),
                 reg(M + 10 * KB + 576, 12, F32), reg(M + 10 * KB + 640, 12, F32),
                 reg(211968, 12, F32), reg(212032, 12, F32),
                 reg(212096, 12, F32), reg(212160, 12, F32)]
    aggr_sb2 = [reg(M + 896, 2, F32), reg(M + 3264, 2, F32),
                reg(M + 10 * KB + 704, 2, F32), reg(M + 10 * KB + 736, 2, F32),
                reg(212224, 2, F32), reg(212256, 2, F32),
                reg(212288, 2, F32), reg(212320, 2, F32)]
    std_sb2 = [reg(M + 928, 1, F32), reg(M + 3296, 1, F32),
               reg(M + 10 * KB + 768, 1, F32), reg(M + 10 * KB + 800, 1, F32),
               reg(212352, 1, F32), reg(212384, 1, F32),
               reg(212416, 1, F32), reg(212448, 1, F32)]
    rstd_sb2 = [reg(M + 960, 1, F32), reg(M + 3328, 1, F32),
                reg(M + 10 * KB + 832, 1, F32), reg(M + 10 * KB + 864, 1, F32),
                reg(212480, 1, F32), reg(212512, 1, F32),
                reg(212544, 1, F32), reg(212576, 1, F32)]
    stats_sb = stats_sb2[0]
    aggr_sb = reg(M + 896, 2, F32)
    std_sb = reg(M + 928, 1, F32)
    rstd_sb = reg(M + 960, 1, F32)
    wg_sb = reg(M + KB, 64, BF16)
    b2_sb = reg(M + KB + 128, 1024, BF16, parts=8)
    gt_sb = reg(M + 10 * KB, 128, BF16)
    gexp_sb = reg(M + 3 * KB + 512, 128, F32)
    grec_sb = reg(M + 4 * KB, 16, F32)
    grec2_sb = reg(M + 4 * KB + 64, 16, F32)
    rsum_sb = reg(M + 4 * KB + 128, 512, BF16)
    bcast_sb = reg(M + 5 * KB + 256, 512, BF16)
    prod_sb = reg(M + 6 * KB + 512, 512, F32)
    grow_sb2 = [reg(M + 9 * KB, 512, BF16, parts=1),
                reg(M + 14 * KB, 512, BF16, parts=1)]

    ps_h = ctx.enter_context(nc.psum_tensor("ps", [128, 4096], F32))
    ps = ps_h[:, :]
    psb = ps.bitcast(BF16)

    def bank(b, cols=512, dtype=F32, parts=128, p0=0):
        if dtype == F32:
            return ps[p0:p0 + parts, b * 512: b * 512 + cols]
        return psb[p0:p0 + parts, b * 1024: b * 1024 + cols]

    sems = {}
    for name in ("sWL", "sXL", "sST", "sMV", "sPE", "sVE", "sSC", "sGP", "sVB"):
        sems[name] = ctx.enter_context(nc.semaphore(name))
    sch = Sched()
    for name in sems:
        sch.sem(name)
    E = sch.emit

    def tap(name, src_ap, waits):
        if not DEBUG or (TAPS is not None and name not in TAPS):
            return 0
        def fn(sy, name=name, src_ap=src_ap):
            return sy.dma_start(dbg[name], src_ap)
        return E("sync", fn, waits=waits, inc=("sST", 16))

    ev = E("gpsimd", lambda g: g.memset(ident, 0.0), inc=("sGP", 1))
    E("gpsimd", lambda g: g.affine_select(
        out=ident, in_=ident, compare_op=OP.not_equal, fill=1.0,
        base=0, pattern=[[-1, 128]], channel_multiplier=1),
        waits=[("sGP", ev)], inc=("sGP", 1))
    ev = E("gpsimd", lambda g: g.memset(onespad, 0.0), inc=("sGP", 1))
    E("gpsimd", lambda g: g.memset(onespad[:, 0:1], 1.0),
      waits=[("sGP", ev)], inc=("sGP", 1))
    E("gpsimd", lambda g: g.memset(ones1, 1.0), inc=("sGP", 1))
    E("gpsimd", lambda g: g.memset(negB, EXP_SHIFT), inc=("sGP", 1))
    E("gpsimd", lambda g: g.memset(eps_sb, EPS), inc=("sGP", 1))
    E("gpsimd", lambda g: g.memset(zero_sb, 0.0), inc=("sGP", 1))
    gp_consts = sch.cnt["sGP"]

    def wload(fn):
        return E("gpsimd", fn, inc=("sWL", 16))

    wl_wv = wload(lambda g: g.dma_start(
        wv_sb.rearrange("p (f o) -> p f o", f=NF),
        wv_d.rearrange("(f p) o -> p f o", p=128)))
    wl_w1 = wload(lambda g: g.dma_start(
        w1_sb.rearrange("p (e f h) -> p e f h", e=NE, f=NF), w1_d))
    wl_w2 = wload(lambda g: g.dma_start(
        w2_sb.rearrange("p (e o) -> p e o", e=NE), w2_d))
    wl_wg = wload(lambda g: g.dma_start(
        wg_sb.rearrange("p (f e) -> p f e", f=NF), wg_d))
    wl_b2 = wload(lambda g: g.dma_start(b2_sb, b2_d))
    wl_bq = wload(lambda g: g.dma_start(bqs, bq_d))
    wl_bq = wload(lambda g: g.dma_start(bks, bk_d))
    wl_bv = wload(lambda g: g.dma_start(bvs, bv_d))
    wl_b1 = wload(lambda g: g.dma_start(b1s, b1_d))
    wl_bg = wload(lambda g: g.dma_start(bgs, bg_d))

    wl_chunk = [0] * NF

    def load_chunk(j, waits):
        def fq(g, j=j):
            return g.dma_start(
                chnk[j % 2][0].rearrange("p (f o) -> p f o", f=NF),
                wqj_d[j].rearrange("(f p) o -> p f o", p=128))
        E("gpsimd", fq, waits=waits, inc=("sWL", 16))

        def fk(g, j=j):
            return g.dma_start(
                chnk[j % 2][1].rearrange("p (f o) -> p f o", f=NF),
                wkj_d[j].rearrange("(f p) o -> p f o", p=128))
        wl_chunk[j] = E("gpsimd", fk, inc=("sWL", 16))

    load_chunk(0, ())
    load_chunk(1, ())

    x_tiled = x_d.rearrange("(t p) d -> t p d", p=128)
    out_tiled = out_d.rearrange("(t p) d -> t p d", p=128)
    xl_ev = {}

    def load_x(t, waits):
        def fn(sy, t=t):
            return sy.dma_start(xroll[t % 2], x_tiled[t])
        xl_ev[t] = E("sync", fn, waits=waits, inc=("sXL", 16))

    load_x(0, ())
    load_x(1, ())

    ln_state = {"sqrt": [0] * 8, "slot": 0, "cons": [0] * 8}

    def emit_ln_stats(src_ap, waits):
        slot = ln_state["slot"]
        ln_state["slot"] = (slot + 1) % 8
        st_b, ag_b = stats_sb2[slot], aggr_sb2[slot]
        sd_b, rs_b = std_sb2[slot], rstd_sb2[slot]
        for cch in range(2):
            w = (list(waits) + [("sSC", ln_state["sqrt"][slot])]) if cch == 0 else ()
            E("vector", lambda v, s=src_ap, c=cch, st=st_b: v.bn_stats(
                st[:, 6 * c:6 * c + 6], s[:, 512 * c:512 * c + 512]),
                waits=w, inc=("sVB", 1))
        vev = E("vector", lambda v, ag=ag_b, st=st_b: v.bn_aggr(ag, st),
                waits=[("sVB", sch.cnt["sVB"])], inc=("sVE", 1))
        E("scalar", lambda s, ag=ag_b, sd=sd_b: s.activation(
            sd, ag[:, 1:2], ACT.Ln, bias=eps_sb),
            waits=[("sVE", vev), ("sGP", gp_consts),
                   ("sVE", ln_state["cons"][slot])], inc=("sSC", 1))
        sev = E("scalar", lambda s, sd=sd_b, rs=rs_b: s.activation(
            rs, sd, ACT.Exp, scale=-0.5, bias=zero_sb), inc=("sSC", 1))
        ln_state["sqrt"][slot] = sev

        def consumed(ev, slot=slot):
            ln_state["cons"][slot] = ev
        return sev, ag_b, rs_b, consumed

    pe_tp1 = [0, 0]
    ve_ev1 = [0, 0]
    ve_vev = [0, 0]
    xq = [xroll[0], xroll[1], oroll[0], oroll[1]]

    def emit_p1_tail(t, vap):
        par = t % 2
        for fc in range(NF):
            w = [("sVE", vap), ("sVE", ve_ev1[par]), ("sGP", gp_consts)] \
                if fc == 0 else ()
            pe_tp1[par] = E("tensor", lambda tt_, fc=fc, par=par: tt_.transpose(
                bank(par, 1024, BF16)[:, 128 * fc:128 * fc + 128],
                ln1tm[par][:, 128 * fc:128 * fc + 128], ident),
                waits=w, inc=("sPE", 1))
        dst = ln1_fm.rearrange("p (f s) -> p f s", f=NF)[:, :, 128 * t:128 * t + 128]
        ve_ev1[par] = E("vector", lambda v, dst=dst, par=par: v.tensor_copy(
            dst, bank(par, 1024, BF16).rearrange("p (f s) -> p f s", f=NF)),
            waits=[("sPE", pe_tp1[par])], inc=("sVE", 1))
        for oh in range(2):
            b = 2 + oh
            w0 = [("sVE", ve_ev1[par]), ("sWL", wl_wv), ("sVE", ve_vev[oh])]
            for fc in range(NF):
                stat = ln1_fm[:, fc * 2048 + 128 * t: fc * 2048 + 128 * t + 128]
                mov = wv_sb[:, fc * 1024 + 512 * oh: fc * 1024 + 512 * oh + 512]
                pe_v = E("tensor", lambda t_, s=stat, m=mov, b=b, fc=fc: t_.matmul(
                    bank(b), s, m, start=(fc == 0), stop=(fc == NF - 1)),
                    waits=w0 if fc == 0 else (), inc=("sPE", 1))
            dst = v_sb[:, t * 1024 + 512 * oh: t * 1024 + 512 * oh + 512]
            ve_vev[oh] = E("vector", lambda v, d=dst, b=b: v.tensor_copy(
                d, bank(b)), waits=[("sPE", pe_v)], inc=("sVE", 1))

    def load_x1(t, waits):
        def fn(sy, t=t):
            return sy.dma_start(xq[t % 4], x_tiled[t])
        xl_ev[t] = E("sync", fn, waits=waits, inc=("sXL", 16))
    load_x1(2, ())
    load_x1(3, ())
    for grp in range(NT // 4):
        ln_p1 = {}
        for q in range(4):
            t = 4 * grp + q
            ln_p1[q] = emit_ln_stats(xq[t % 4], [("sXL", xl_ev[t])])
        for q in range(4):
            t = 4 * grp + q
            par = t % 2
            rstd_ev, ag_a, rs_a, cons = ln_p1[q]
            vap = E("vector", lambda v, t=t, ag=ag_a, rs=rs_a: v.tensor_scalar(
                ln1tm[t % 2], xq[t % 4], ag[:, 0:1], rs,
                op0=OP.subtract, op1=OP.mult),
                waits=[("sSC", rstd_ev), ("sPE", pe_tp1[par])], inc=("sVE", 1))
            cons(vap)
            if t + 4 < NT:
                load_x1(t + 4, [("sVE", vap)])
            emit_p1_tail(t, vap)

    ln1_done = max(ve_ev1)
    v_done = max(ve_vev)

    att_exp_last = 0
    qkv_bank_ev = [ve_ev1[0], ve_ev1[1], ve_vev[0], ve_vev[1]]
    exp_hist = {}
    ve_pack_hist = {}
    ve_bc_hist = {}
    ve_prod_hist = {}
    pend = {"a": None, "b": None}
    gslot = 0
    att_done = 0

    def emit_pv(j, c, w, kc):
        p4 = kc % 4
        ob = 4 + (w % 2)
        sb_bank = 6 + (w % 2)
        for h in (0, 1):
            vstat = v_sb[:, kc * 1024 + 128 * j + 64 * h:
                         kc * 1024 + 128 * j + 64 * h + 64]
            mov = ptsb[p4][:, 512 * h:512 * h + 512]
            wts = [("sSC", exp_hist[j * 64 + c * 16 + kc])]
            if kc == 0 and w >= 2 and (w - 2) in ve_pack_hist:
                wts.append(("sVE", ve_pack_hist[w - 2]))
            E("tensor", lambda t_, vs=vstat, m=mov, h=h, kc=kc, ob=ob:
              t_.matmul(bank(ob, 512, F32, 64, 64 * h), vs, m,
                        start=(kc == 0), stop=(kc == NT - 1),
                        skip_group_check=True),
              waits=wts if h == 0 else (), inc=("sPE", 1))
        for h in (0, 1):
            mov = ptsb[p4][:, 512 * h:512 * h + 512]
            wts2 = []
            if h == 0 and kc == 0 and w >= 2 and (w - 2) in ve_bc_hist:
                wts2.append(("sVE", ve_bc_hist[w - 2]))
            E("tensor", lambda t_, m=mov, h=h, kc=kc, b=sb_bank:
              t_.matmul(bank(b, 512, F32, 64, 64 * h),
                        onespad[:, 0:64], m,
                        start=(kc == 0), stop=(kc == NT - 1),
                        skip_group_check=True),
              waits=wts2, inc=("sPE", 1))

    pe_rank_hist = {}

    def emit_norm_a(j, w, ve_recip):
        par = w % 2
        sb_b = 6 + par
        ob_b = 4 + par
        pe_b = E("tensor", lambda t_, b=sb_b: t_.matmul(
            bank(b, 512, F32, 64, 0), ones1[0:1, 0:64], rsum_sb[0:1, :],
            start=True, stop=True, skip_group_check=True),
            waits=[("sSC", ve_recip)], inc=("sPE", 1))
        pe_b = E("tensor", lambda t_, b=sb_b: t_.matmul(
            bank(b, 512, F32, 64, 64), ones1[64:65, 0:64], rsum_sb[64:65, :],
            start=True, stop=True, skip_group_check=True), inc=("sPE", 1))
        pe_rank_hist[w] = pe_b
        ve_bc = E("vector", lambda v, b=sb_b: v.tensor_copy(bcast_sb, bank(b)),
                  waits=[("sPE", pe_b)], inc=("sVE", 1))
        ve_bc_hist[w] = ve_bc
        vev = E("vector", lambda v, b=ob_b: v.scalar_tensor_tensor(
            prod_sb, bank(b), 1.0, bcast_sb, op0=OP.mult, op1=OP.mult),
            waits=[("sVE", ve_bc)], inc=("sVE", 1))
        ve_prod = E("vector", lambda v, j=j: v.tensor_scalar(
            bcast_sb, prod_sb, bvs[:, j:j + 1], None, op0=OP.add),
            waits=[("sVE", vev), ("sWL", wl_bv)], inc=("sVE", 1))
        ve_prod_hist[w] = ve_prod

    def emit_norm_b(j, c, w):
        par = w % 2
        ob_b = 4 + par
        pe_tp = 0
        for qt in range(4):
            wts = [("sVE", ve_prod_hist[w])] if qt == 0 else ()
            pe_tp = E("tensor", lambda t_, qt=qt, b=ob_b: t_.transpose(
                bank(b, 1024, BF16)[:, 128 * qt:128 * qt + 128],
                bcast_sb[:, 128 * qt:128 * qt + 128], ident),
                waits=wts, inc=("sPE", 1))
        dst = attnout.rearrange("p (t f) -> p t f", t=NT)[
            :, 4 * c:4 * c + 4, 128 * j:128 * j + 128]
        ve_pack_hist[w] = E("vector", lambda v, d=dst, b=ob_b: v.tensor_copy(
            d, bank(b, 512, BF16).rearrange("p (t f) -> p t f", t=4)),
            waits=[("sPE", pe_tp)], inc=("sVE", 1))

    for j in range(NF):
        for (pidx, bias_ap, dest) in ((0, bqs, q_sb), (1, bks, k_sb)):
            for c in range(4):
                b = (pidx * 4 + c) % 4
                w0 = [("sWL", wl_chunk[j]), ("sVE", ln1_done),
                      ("sVE", qkv_bank_ev[b]), ("sSC", att_exp_last)]
                for fc in range(NF):
                    stat = chnk[j % 2][pidx][:, fc * 128: fc * 128 + 128]
                    mov = ln1_fm[:, fc * 2048 + 512 * c: fc * 2048 + 512 * c + 512]
                    pe_ev = E("tensor", lambda t_, s=stat, m=mov, b=b, fc=fc:
                              t_.matmul(bank(b), s, m,
                                        start=(fc == 0), stop=(fc == NF - 1)),
                              waits=w0 if fc == 0 else (), inc=("sPE", 1))
                dst = dest[:, 512 * c: 512 * c + 512]
                qkv_bank_ev[b] = E("vector", lambda v, d=dst, b=b, ba=bias_ap, j=j:
                                   v.tensor_scalar(d, bank(b), ba[:, j:j + 1], None,
                                                   op0=OP.add),
                                   waits=[("sPE", pe_ev), ("sWL", wl_bq)],
                                   inc=("sVE", 1))
            if pidx == 0 and pend["a"] is not None:
                pend["a"]()
                pend["a"] = None
            if pidx == 1 and pend["b"] is not None:
                pend["b"]()
                pend["b"] = None
        qk_done = max(qkv_bank_ev)
        if j + 2 < NF:
            load_chunk(j + 2, [("sPE", pe_ev)])

        for c in range(4):
            w = j * 4 + c
            par_w = w % 2
            ob = 4 + par_w
            sb_bank = 6 + par_w
            q0 = 512 * c
            prev_slot = {}
            for kc in range(NT):
                par = kc % 2
                pb = 2 * par
                g = gslot
                gslot += 1
                wts = [("sVE", qk_done)]
                if g - 2 in exp_hist:
                    wts.append(("sSC", exp_hist[g - 2]))
                if j == 0 and c == 0 and kc < 2:
                    wts.append(("sVE", v_done))
                for h in (0, 1):
                    r0 = 64 * h
                    stat = k_sb[r0:r0 + 64, 128 * kc:128 * kc + 128]
                    mov = q_sb[r0:r0 + 64, q0:q0 + 512]
                    pe_st = E("tensor", lambda t_, s=stat, m=mov, pb=pb, h=h:
                              t_.matmul(bank(pb + h), s, m, start=True, stop=True),
                              waits=wts if h == 0 else (), inc=("sPE", 1))
                if kc == 4 and pend["a"] is not None:
                    pend["a"]()
                    pend["a"] = None
                if kc == 12 and pend["b"] is not None:
                    pend["b"]()
                    pend["b"] = None
                if kc > 1:
                    emit_pv(j, c, w, kc - 2)
                sc_ev = E("scalar", lambda s_, pb=pb, p4=kc % 4: s_.activation(
                    ptsb[p4], bank(pb, 1024), ACT.Exp, bias=negB, scale=0.125),
                    waits=[("sPE", pe_st), ("sGP", gp_consts)], inc=("sSC", 1))
                exp_hist[g] = sc_ev
                att_exp_last = sc_ev
            emit_pv(j, c, w, NT - 2)
            emit_pv(j, c, w, NT - 1)
            pe_sums_last = sch.cnt["sPE"]

            wr = [("sPE", pe_sums_last)]
            if (w - 1) in pe_rank_hist:
                wr.append(("sPE", pe_rank_hist[w - 1]))
            if (w - 1) in ve_prod_hist:
                wr.append(("sVE", ve_prod_hist[w - 1]))

            def f_ln_a(s, b=sb_bank):
                return s.activation(prod_sb[0:1, :], bank(b, 512, F32, 1, 0),
                                    ACT.Ln, bias=zero_sb[0:1, :])

            def f_ex_a(s):
                with nc.allow_low_precision(reason="rsum feeds bf16 matmul"):
                    return s.activation(rsum_sb[0:1, :], prod_sb[0:1, :],
                                        ACT.Exp, scale=-1.0, bias=zero_sb[0:1, :])

            def f_ln_b(s, b=sb_bank):
                return s.activation(prod_sb[64:65, :], bank(b, 512, F32, 1, 64),
                                    ACT.Ln, bias=zero_sb[64:65, :])

            def f_ex_b(s):
                with nc.allow_low_precision(reason="rsum feeds bf16 matmul"):
                    return s.activation(rsum_sb[64:65, :], prod_sb[64:65, :],
                                        ACT.Exp, scale=-1.0, bias=zero_sb[64:65, :])
            E("scalar", f_ln_a, waits=wr, inc=("sSC", 1))
            E("scalar", f_ex_a, inc=("sSC", 1))
            E("scalar", f_ln_b, inc=("sSC", 1))
            ve_recip = E("scalar", f_ex_b, inc=("sSC", 1))

            def mk_pend(j=j, c=c, w=w, ve_recip=ve_recip):
                def fa():
                    emit_norm_a(j, w, ve_recip)

                def fb():
                    emit_norm_b(j, c, w)
                return fa, fb
            pend["a"], pend["b"] = mk_pend()
    pend["a"]()
    pend["b"]()
    att_done = ve_pack_hist[NF * 4 - 1]

    tap("d_att", attnout, [("sVE", att_done)])

    tp_ev = [0, 0]
    tp_pe = [0, 0]
    tp_next = [0]

    def load_x4(t, waits):
        def fn(sy, t=t):
            return sy.dma_start(xroll[t % 2], x_tiled[t])
        xl_ev[t] = E("sync", fn, waits=waits, inc=("sXL", 16))

    att_exp_final = att_exp_last
    load_x4(0, [("sVE", ve_pack_hist[28])])
    load_x4(1, [("sVE", ve_pack_hist[28])])

    ve_ln2win = [0, 0, 0, 0]
    ve_glog = 0
    ve_gfm_prev = 0
    gfm_evs = {}
    pe_rank = 0
    pe_rank_hist2 = {}
    pe_down_last = 0
    ve_hp = {}
    ve_dev = [0, 0]
    st_ev = [0] * NT
    pair_read = [0, 0]
    vadd_ev = {}
    ln_b_all = {}

    def preABC(c):
        t0 = 4 * c
        ln_a = {}
        for q in range(4):
            a_t = attnout[:, 1024 * (t0 + q):1024 * (t0 + q) + 1024]
            ln_a[q] = emit_ln_stats(a_t, [("sVE", ve_pack_hist[28 + c])])
        for q in range(4):
            t = t0 + q
            a_t = attnout[:, 1024 * t:1024 * t + 1024]
            rstd_ev, ag_a, rs_a, cons = ln_a[q]
            vev = E("vector", lambda v, a=a_t, ag=ag_a, rs=rs_a:
                    v.tensor_scalar(
                lnt_sb, a, ag[:, 0:1], rs, op0=OP.subtract, op1=OP.mult),
                waits=[("sSC", rstd_ev)], inc=("sVE", 1))
            cons(vev)
            x2_t = x2_sb[:, 1024 * t:1024 * t + 1024]
            vadd = E("vector", lambda v, x2=x2_t, xp=t % 2: v.tensor_add(
                x2, xroll[xp], lnt_sb),
                waits=[("sVE", vev), ("sXL", xl_ev[t])], inc=("sVE", 1))
            if t + 2 < NT:
                load_x4(t + 2, [("sVE", vadd)])
            vadd_ev[t] = vadd
        ln_b = {}
        for q in range(4):
            t = t0 + q
            ln_b[q] = emit_ln_stats(
                x2_sb[:, 1024 * t:1024 * t + 1024], [("sVE", vadd_ev[t])])
        ln_b_all[c] = ln_b

    preABC(0)

    def emit_D_pair(c, pair):
        ln_b = ln_b_all[c]
        t0 = 4 * c
        par_of = {}

        def p4_d_ap(q):
            t = t0 + q
            par_of[q] = tp_next[0] % 2
            tp_next[0] += 1
            rstd_ev, ag_a, rs_a, cons = ln_b[q]
            x2_t = x2_sb[:, 1024 * t:1024 * t + 1024]
            vap = E("vector", lambda v, x2=x2_t, lp=t % 2, ag=ag_a, rs=rs_a:
                    v.tensor_scalar(
                ln1tm[lp], x2, ag[:, 0:1], rs,
                op0=OP.subtract, op1=OP.mult),
                waits=[("sSC", rstd_ev), ("sPE", tp_pe[par_of[q]])],
                inc=("sVE", 1))
            cons(vap)
            return vap

        def p4_d_tp(q, vap):
            t = t0 + q
            par = par_of[q]
            for fc in range(NF):
                w = [("sVE", vap), ("sVE", tp_ev[par]),
                     ("sSC", att_exp_final)] if fc == 0 else ()
                tp_pe[par] = E("tensor", lambda t_, fc=fc, par=par, lp=t % 2:
                               t_.transpose(
                    bank(par, 1024, BF16)[:, 128 * fc:128 * fc + 128],
                    ln1tm[lp][:, 128 * fc:128 * fc + 128], ident),
                    waits=w, inc=("sPE", 1))
            dst = ln2_fm.rearrange("p (f s) -> p f s", f=NF)[
                :, :, 128 * t:128 * t + 128]
            tp_ev[par] = E("vector", lambda v, dst=dst, par=par: v.tensor_copy(
                dst, bank(par, 1024, BF16).rearrange("p (f s) -> p f s", f=NF)),
                waits=[("sPE", tp_pe[par])], inc=("sVE", 1))
            ve_ln2win[c] = max(ve_ln2win[c], tp_ev[par])

        vaps = [p4_d_ap(2 * pair), p4_d_ap(2 * pair + 1)]
        p4_d_tp(2 * pair, vaps[0])
        p4_d_tp(2 * pair + 1, vaps[1])

    emit_D_pair(0, 0)
    for c in range(4):
        emit_D_pair(c, 1)

        w0g = [("sVE", ve_ln2win[c]), ("sWL", wl_wg), ("sVE", ve_gfm_prev)]
        if c == 0:
            w0g.append(("sVE", ve_pack_hist[30]))
        for fc in range(NF):
            stat = wg_sb[:, fc * 8: fc * 8 + 8]
            mov = ln2_fm[:, fc * 2048 + 512 * c: fc * 2048 + 512 * c + 512]
            pe_g = E("tensor", lambda t_, s=stat, m=mov, fc=fc: t_.matmul(
                bank(4, 512, F32, 8, 0), s, m, start=(fc == 0), stop=(fc == NF - 1)),
                waits=w0g if fc == 0 else (), inc=("sPE", 1))
        ve_glog = E("vector", lambda v: v.tensor_scalar(
            glog_win, bank(4, 512, F32, 8, 0),
            bgs, None, op0=OP.add),
            waits=[("sPE", pe_g), ("sWL", wl_bg)], inc=("sVE", 1))
        pe_h_pre = {}
        for e in (0, 1):
            hb = 2 + e % 2
            prior = ve_hp.get((e - 2) % NE, 0) if c > 0 else 0
            w0 = [("sVE", ve_ln2win[c]), ("sWL", wl_w1), ("sVE", prior),
                  ("sSC", att_exp_final)]
            if c > 0:
                w0.append(("sVE", pair_read[1]))
            for fc in range(NF):
                stat = w1_sb[:, (e * 8 + fc) * 128:(e * 8 + fc) * 128 + 128]
                mov = ln2_fm[:, fc * 2048 + 512 * c: fc * 2048 + 512 * c + 512]
                pe_h_pre[e] = E("tensor", lambda t_, s=stat, m=mov, hb=hb, fc=fc:
                                t_.matmul(
                    bank(hb), s, m, start=(fc == 0), stop=(fc == NF - 1)),
                    waits=w0 if fc == 0 else (), inc=("sPE", 1))
        pe_gt = 0
        for q in range(4):
            cc = 4 * c + q
            w = [("sVE", ve_glog)] if q == 0 else ()
            pe_gt = E("tensor", lambda t_, q=q: t_.transpose(
                bank(4, 1024, BF16)[:, 512 + 8 * q:512 + 8 * q + 8],
                glog_win[:, 128 * q:128 * q + 128], ident[0:8, 0:8]),
                waits=w, inc=("sPE", 1))
        ve_gt = E("vector", lambda v, c=c: v.tensor_copy(
            gt_sb[:, 32 * c:32 * c + 32], bank(4, 1024, BF16)[:, 512:512 + 32]),
            waits=[("sPE", pe_gt)], inc=("sVE", 1))
        sc_g = E("scalar", lambda s, c=c: s.activation(
            gexp_sb[:, 32 * c:32 * c + 32], gt_sb[:, 32 * c:32 * c + 32],
            ACT.Exp, bias=zero_sb),
            waits=[("sVE", ve_gt)], inc=("sSC", 1))
        ve_gtm = 0
        for q in range(4):
            tt = 4 * c + q
            vev = E("vector", lambda v, tt=tt: v.tensor_scalar(
                gacc_sb, gexp_sb[:, 8 * tt:8 * tt + 8], 1.0, None,
                op0=OP.mult, op1=OP.add, accum_out=grec_sb[:, tt:tt + 1]),
                waits=[("sSC", sc_g)], inc=("sVE", 1))
            E("scalar", lambda s, tt=tt: s.activation(
                std_sb, grec_sb[:, tt:tt + 1], ACT.Ln, bias=zero_sb),
                waits=[("sVE", vev)], inc=("sSC", 1))
            scv = E("scalar", lambda s, tt=tt: s.activation(
                grec2_sb[:, tt:tt + 1], std_sb, ACT.Exp, scale=-1.0, bias=zero_sb),
                inc=("sSC", 1))
            ve_gtm = E("vector", lambda v, tt=tt: v.tensor_scalar(
                gt_sb[:, 8 * tt:8 * tt + 8], gexp_sb[:, 8 * tt:8 * tt + 8],
                grec2_sb[:, tt:tt + 1], None, op0=OP.mult),
                waits=[("sSC", scv)], inc=("sVE", 1))
        pe_gf = 0
        for q in range(4):
            tt = 4 * c + q
            w = [("sVE", ve_gtm)] if q == 0 else ()
            pe_gf = E("tensor", lambda t_, tt=tt, q=q: t_.transpose(
                bank(4, 1024, BF16, 8, 0)[:, 128 * q:128 * q + 128],
                gt_sb[:, 8 * tt:8 * tt + 8], ident),
                waits=w, inc=("sPE", 1))
        wgf = [("sPE", pe_gf)]
        if c > 0:
            wgf.append(("sMV", mv_grow_last))
        ve_gfm_prev = E("vector", lambda v: v.tensor_copy(
            gfm_win, bank(4, 1024, BF16, 8, 0)[:, 0:512]),
            waits=wgf, inc=("sVE", 1))
        gfm_evs[c] = ve_gfm_prev

        for e in range(NE):
            mv_grow = E("sync", lambda sy, e=e: sy.dma_start(
                grow_sb2[e % 2], gfm_win[e:e + 1, :]),
                waits=[("sVE", gfm_evs[c]), ("sPE", pe_rank_hist2.get((e - 2) % NE, 0))],
                inc=("sMV", 16))
            mv_grow_last = mv_grow
            hb = 2 + e % 2
            prior = ve_hp.get((e - 2) % NE, 0) if (e >= 2 or c > 0) else 0
            w0 = [("sVE", ve_ln2win[c]), ("sWL", wl_w1), ("sVE", prior),
                  ("sSC", att_exp_final)]
            if e < 2 and c > 0:
                w0.append(("sVE", pair_read[1]))
            for fc in range(NF):
                stat = w1_sb[:, (e * 8 + fc) * 128:(e * 8 + fc) * 128 + 128]
                mov = ln2_fm[:, fc * 2048 + 512 * c: fc * 2048 + 512 * c + 512]
                pe_h = E("tensor", lambda t_, s=stat, m=mov, hb=hb, fc=fc: t_.matmul(
                    bank(hb), s, m, start=(fc == 0), stop=(fc == NF - 1)),
                    waits=w0 if fc == 0 else (), inc=("sPE", 1))
            wr1 = [("sMV", mv_grow), ("sVE", prior)]
            if c == 0 and e == 0:
                wr1.append(("sVE", ve_pack_hist[31]))
            pe_rank = E("tensor", lambda t_, e=e: t_.matmul(
                bank(5, 512, F32, 128, 0), ones1[0:1, :], grow_sb2[e % 2],
                start=True, stop=True), waits=wr1, inc=("sPE", 1))
            pe_rank_hist2[e] = pe_rank
            hw = [("sPE", pe_h), ("sWL", wl_b1), ("sPE", pe_down_last)]
            if (e - 2) % NE in ve_hp and (e >= 2 or c > 0):
                hw.append(("sVE", ve_hp[(e - 2) % NE]))
            vev = E("scalar", lambda s, hb=hb, e=e: s.activation(
                htmp_sb2[e % 2], bank(hb), ACT.Relu, bias=b1s[:, e:e + 1]),
                waits=hw, inc=("sSC", 1))
            ve_hp[e] = E("vector", lambda v, e=e: v.scalar_tensor_tensor(
                hp_sb[e], htmp_sb2[e % 2], 1.0, bank(5, 512),
                op0=OP.mult, op1=OP.mult),
                waits=[("sSC", vev), ("sPE", pe_rank)], inc=("sVE", 1))
        down_done = {}

        def emit_down(tc):
            b0 = (6, 2)[tc % 2]
            for half in range(2):
                for e in range(NE):
                    wts = []
                    if half == 0:
                        if e == 0:
                            wts = [("sWL", wl_w2), ("sVE", ve_hp[0])]
                            if tc % 2 == 0:
                                if c == 0 and tc == 0:
                                    wts += [("sVE", ve_bc_hist[30]),
                                            ("sVE", ve_bc_hist[31])]
                                else:
                                    wts.append(("sVE", pair_read[0]))
                            else:
                                wts += [("sVE", ve_hp[6]), ("sVE", ve_hp[7]),
                                        ("sVE", pair_read[1])]
                        else:
                            wts = [("sVE", ve_hp[e])]
                    stat = hp_sb[e][:, 128 * tc:128 * tc + 128]
                    mov = w2_sb[:, e * 1024 + 512 * half:
                                e * 1024 + 512 * half + 512]
                    E("tensor", lambda t_, s=stat, m=mov, b=b0 + half, e=e:
                      t_.matmul(bank(b), s, m, start=(e == 0), stop=False),
                      waits=wts, inc=("sPE", 1))
                last = E("tensor", lambda t_, tc=tc, b=b0 + half, half=half:
                         t_.matmul(
                    bank(b), gfm_win[:, 128 * tc:128 * tc + 128],
                    b2_sb[:, 512 * half:512 * half + 512],
                    start=False, stop=True),
                    waits=[("sWL", wl_b2), ("sVE", gfm_evs[c])], inc=("sPE", 1))
            down_done[tc] = last

        def emit_p6(tc):
            t = 4 * c + tc
            view = bank((6, 2)[tc % 2], 1024, F32)
            rstd_ev, ag_a, rs_a, cons = emit_ln_stats(
                view, [("sPE", down_done[tc])])
            vev = E("vector", lambda v, view=view, ag=ag_a, rs=rs_a:
                    v.tensor_scalar(
                lnt_sb, view, ag[:, 0:1], rs, op0=OP.subtract, op1=OP.mult),
                waits=[("sSC", rstd_ev)], inc=("sVE", 1))
            cons(vev)
            pair_read[tc % 2] = vev
            x2_t = x2_sb[:, 1024 * t:1024 * t + 1024]
            w = [("sVE", vev)]
            if t >= 2:
                w.append(("sST", st_ev[t - 2]))
            vout = E("vector", lambda v, x2=x2_t, op=t % 2: v.tensor_add(
                oroll[op], x2, lnt_sb), waits=w, inc=("sVE", 1))
            st_ev[t] = E("sync", lambda sy, t=t, op=t % 2: sy.dma_start(
                out_tiled[t], oroll[op]), waits=[("sVE", vout)], inc=("sST", 16))

        emit_down(0)
        emit_down(1)
        emit_p6(0)
        emit_down(2)
        emit_p6(1)
        emit_down(3)
        pe_down_last = down_done[3]
        if c + 1 < 4:
            preABC(c + 1)
            emit_D_pair(c + 1, 0)
        emit_p6(2)
        emit_p6(3)
    final_st = sch.cnt["sST"]
    E("sync", lambda sy: sy.wait_ge(sems["sST"], final_st), waits=())

    block = ctx.enter_context(nc.Block())

    def mk(name):
        def body(eng):
            run_engine(eng, name, sch.ops[name], sems)
        return body

    block.sync(mk("sync"))
    block.tensor(mk("tensor"))
    block.vector(mk("vector"))
    block.scalar(mk("scalar"))
    block.gpsimd(mk("gpsimd"))
    ctx.close()
    return nc


def host_layouts(inputs):
    f32 = np.float32
    Wq = np.asarray(inputs["L0_attn_Wq"], f32)
    Wk = np.asarray(inputs["L0_attn_Wk"], f32)
    Wv = np.asarray(inputs["L0_attn_Wv"], f32)
    W1 = np.asarray(inputs["L0_ffn_W1"], f32)
    W2 = np.asarray(inputs["L0_ffn_W2"], f32)
    Wg = np.asarray(inputs["L0_ffn_Wg"], f32)
    d = {}
    wqT = np.ascontiguousarray(Wq.T)
    wkT = np.ascontiguousarray(Wk.T)
    d["wqj"] = np.ascontiguousarray(wqT.reshape(D, NF, 128).transpose(1, 0, 2))
    d["wkj"] = np.ascontiguousarray(wkT.reshape(D, NF, 128).transpose(1, 0, 2))
    d["wv"] = np.ascontiguousarray(Wv.T)
    d["w1l"] = np.ascontiguousarray(W1.reshape(NE, NF, 128, HD).transpose(2, 0, 1, 3))
    d["w2l"] = np.ascontiguousarray(W2.transpose(1, 0, 2))
    d["wgl"] = np.ascontiguousarray(Wg.reshape(NF, 128, NE).transpose(1, 0, 2))
    d["bq2"] = np.ascontiguousarray(np.asarray(inputs["L0_attn_bq"], f32).reshape(NF, 128).T)
    d["bk2"] = np.ascontiguousarray(np.asarray(inputs["L0_attn_bk"], f32).reshape(NF, 128).T)
    d["bv2"] = np.ascontiguousarray(np.asarray(inputs["L0_attn_bv"], f32).reshape(NF, 128).T)
    d["b12"] = np.ascontiguousarray(np.asarray(inputs["L0_ffn_b1"], f32).T)
    d["b2l"] = np.ascontiguousarray(np.asarray(inputs["L0_ffn_b2"], f32))
    d["bg2"] = np.ascontiguousarray(np.asarray(inputs["L0_ffn_bg"], f32).reshape(NE, 1))
    return d


_NC_CACHE = None


def kernel(**inputs):
    global _NC_CACHE
    if _NC_CACHE is None:
        _NC_CACHE = build_nc()
    nc = _NC_CACHE
    shared = host_layouts(inputs)
    x = np.asarray(inputs["L0_attn_x"], np.float32)
    in_maps = []
    for i in range(8):
        m = dict(shared)
        m["x"] = np.ascontiguousarray(x[i])
        in_maps.append(m)
    res = run_bass_kernel_spmd(nc, in_maps, core_ids=list(range(8)))
    if DEBUG:
        kernel.debug = res.results
    return np.stack([np.asarray(r["out"], np.float32) for r in res.results], axis=0)


# version 29
# speedup vs baseline: 1.0358x; 1.0029x over previous
import numpy as np
import concourse.bass as bass
import concourse.mybir as mybir
from concourse.bass_utils import run_bass_kernel_spmd
from contextlib import ExitStack

F32 = mybir.dt.float32
BF16 = mybir.dt.bfloat16
AX = mybir.AxisListType
OP = mybir.AluOpType
ACT = mybir.ActivationFunctionType

S, D, NT, NF, NE, HD = 2048, 1024, 16, 8, 8, 128
DEBUG = False
TAPS = None
EXP_SHIFT = -16.0
EPS = 1e-5
KB = 1024
ARENA_BYTES = 212832
ENGINES = ("sync", "tensor", "vector", "scalar", "gpsimd")
OWN_SEM = {"tensor": "sPE", "vector": "sVE", "scalar": "sSC"}


class Sched:
    def __init__(self):
        self.ops = {e: [] for e in ENGINES}
        self.cnt = {}

    def sem(self, name):
        self.cnt[name] = 0

    def emit(self, eng, fn, waits=(), inc=None):
        waits = [(s, v) for (s, v) in waits if v > 0]
        for s, v in waits:
            assert v <= self.cnt[s], f"wait on future event {s}={v} (now {self.cnt[s]})"
        if inc is not None:
            self.cnt[inc[0]] += inc[1]
        self.ops[eng].append((tuple(waits), fn, inc))
        return self.cnt[inc[0]] if inc is not None else None


def run_engine(eng_obj, eng_name, ops, sems):
    own = OWN_SEM.get(eng_name)
    watermark = {}
    for waits, fn, inc in ops:
        for s, v in waits:
            if s == own or watermark.get(s, 0) >= v:
                continue
            eng_obj.wait_ge(sems[s], v)
            watermark[s] = v
        ins = fn(eng_obj)
        if inc is not None and ins is not None:
            ins.then_inc(sems[inc[0]], inc[1])


def build_nc():
    nc = bass.Bass("TRN2", target_bir_lowering=False, debug=False,
                   detect_race_conditions=False)

    x_d = nc.dram_tensor("x", [S, D], F32, kind="ExternalInput").ap()
    wqj_d = nc.dram_tensor("wqj", [NF, D, 128], F32, kind="ExternalInput").ap()
    wkj_d = nc.dram_tensor("wkj", [NF, D, 128], F32, kind="ExternalInput").ap()
    wv_d = nc.dram_tensor("wv", [D, D], F32, kind="ExternalInput").ap()
    w1_d = nc.dram_tensor("w1l", [128, NE, NF, HD], F32, kind="ExternalInput").ap()
    w2_d = nc.dram_tensor("w2l", [HD, NE, D], F32, kind="ExternalInput").ap()
    wg_d = nc.dram_tensor("wgl", [128, NF, NE], F32, kind="ExternalInput").ap()
    bq_d = nc.dram_tensor("bq2", [128, NF], F32, kind="ExternalInput").ap()
    bk_d = nc.dram_tensor("bk2", [128, NF], F32, kind="ExternalInput").ap()
    bv_d = nc.dram_tensor("bv2", [128, NF], F32, kind="ExternalInput").ap()
    b1_d = nc.dram_tensor("b12", [HD, NE], F32, kind="ExternalInput").ap()
    b2_d = nc.dram_tensor("b2l", [NE, D], F32, kind="ExternalInput").ap()
    bg_d = nc.dram_tensor("bg2", [NE, 1], F32, kind="ExternalInput").ap()
    out_d = nc.dram_tensor("out", [S, D], F32, kind="ExternalOutput").ap()
    dbg = {}
    if DEBUG:
        BD = mybir.dt.bfloat16
        for nm, shape, dt_ in [("d_ln1fm", [128, 16384], BD), ("d_q", [128, 2048], BD),
                               ("d_k", [128, 2048], BD), ("d_v", [128, 16384], BD),
                               ("d_pt0", [128, 1024], BD), ("d_att", [128, 16384], BD),
                               ("d_x2", [128, 16384], F32), ("d_ln2", [128, 16384], BD),
                               ("d_gt", [128, 128], BD), ("d_gfm", [8, 2048], BD),
                               ("d_glog", [8, 2048], BD), ("d_gtpre", [128, 128], BD),
                               ("d_h", [128, 4096], BD), ("d_hp", [128, 4096], BD),
                               ("d_gb", [128, 4096], F32),
                               ("d_gexp", [128, 128], F32), ("d_grec", [128, 16], F32),
                               ("d_ffn", [128, 16384], BD), ("d_rsum", [128, 512], BD),
                               ("d_bc", [128, 512], BD), ("d_prod", [128, 512], F32)]:
            dbg[nm] = nc.dram_tensor(nm, shape, dt_, kind="ExternalOutput").ap()


    ctx = ExitStack()
    arena_h = ctx.enter_context(
        nc.sbuf_tensor("arena", [128, ARENA_BYTES], mybir.dt.uint8))
    arena = arena_h[:, :]

    def reg(off_b, cols, dtype, parts=128):
        nbytes = cols * mybir.dt.size(dtype)
        assert off_b % 32 == 0 and off_b + nbytes <= ARENA_BYTES, (off_b, nbytes)
        return arena[0:parts, off_b:off_b + nbytes].bitcast(dtype)

    w1_sb = reg(0, 8192, BF16)
    w2_sb = reg(16 * KB, 8192, BF16)
    chnk = [(reg(32 * KB + p * 4096, 1024, BF16),
             reg(32 * KB + p * 4096 + 2048, 1024, BF16)) for p in (0, 1)]
    hp_sb = [reg(32 * KB + e * KB, 512, BF16) for e in range(NE)]
    ln1_fm = reg(40 * KB, 16384, BF16)
    x2_sb = reg(40 * KB, 16384, F32)
    wv_sb = reg(72 * KB, 8192, BF16)
    q_sb = reg(88 * KB, 2048, BF16)
    k_sb = reg(92 * KB, 2048, BF16)
    ptsb = [reg(96 * KB + p * 2048, 1024, BF16) for p in (0, 1, 2, 3)]
    v_sb = reg(104 * KB, 16384, BF16)
    ln2_fm = reg(104 * KB, 16384, BF16)
    attnout = reg(136 * KB, 16384, BF16)
    ffn_fm = reg(136 * KB, 16384, BF16)
    xroll = [reg(168 * KB + p * 4096, 1024, F32) for p in (0, 1)]
    oroll = [reg(176 * KB + p * 4096, 1024, F32) for p in (0, 1)]
    lnt_sb = reg(184 * KB, 1024, F32)
    ln1tm = [reg(188 * KB + p * 2048, 1024, BF16) for p in (0, 1)]
    M = 192 * KB
    ident = reg(M, 128, BF16)
    onespad = reg(M + 256, 64, BF16)
    ones1 = reg(M + 384, 128, BF16)
    bqs = reg(M + 640, NF, F32)
    bks = reg(M + 672, NF, F32)
    bvs = reg(M + 704, NF, F32)
    b1s = reg(M + 736, NE, F32, parts=HD)
    bgs = reg(M + 768, 1, F32, parts=8)
    negB = reg(M + 800, 1, F32)
    eps_sb = reg(M + 992, 1, F32)
    zero_sb = reg(M + 10 * KB + 512, 1, F32)
    gacc_sb = reg(M + 10 * KB + 544, 8, BF16)
    htmp_sb2 = [reg(M + 11 * KB, 512, BF16), reg(136 * KB, 512, BF16)]
    glog_win = reg(M + 12 * KB, 512, BF16, parts=8)
    gfm_win = reg(M + 13 * KB, 512, BF16, parts=8)
    stats_sb2 = [reg(M + 832, 12, F32), reg(M + 3200, 12, F32),
                 reg(M + 10 * KB + 576, 12, F32), reg(M + 10 * KB + 640, 12, F32),
                 reg(211968, 12, F32), reg(212032, 12, F32),
                 reg(212096, 12, F32), reg(212160, 12, F32)]
    aggr_sb2 = [reg(M + 896, 2, F32), reg(M + 3264, 2, F32),
                reg(M + 10 * KB + 704, 2, F32), reg(M + 10 * KB + 736, 2, F32),
                reg(212224, 2, F32), reg(212256, 2, F32),
                reg(212288, 2, F32), reg(212320, 2, F32)]
    std_sb2 = [reg(M + 928, 1, F32), reg(M + 3296, 1, F32),
               reg(M + 10 * KB + 768, 1, F32), reg(M + 10 * KB + 800, 1, F32),
               reg(212352, 1, F32), reg(212384, 1, F32),
               reg(212416, 1, F32), reg(212448, 1, F32)]
    rstd_sb2 = [reg(M + 960, 1, F32), reg(M + 3328, 1, F32),
                reg(M + 10 * KB + 832, 1, F32), reg(M + 10 * KB + 864, 1, F32),
                reg(212480, 1, F32), reg(212512, 1, F32),
                reg(212544, 1, F32), reg(212576, 1, F32)]
    stats_sb = stats_sb2[0]
    aggr_sb = reg(M + 896, 2, F32)
    std_sb = reg(M + 928, 1, F32)
    rstd_sb = reg(M + 960, 1, F32)
    wg_sb = reg(M + KB, 64, BF16)
    b2_sb = reg(M + KB + 128, 1024, BF16, parts=8)
    gt_sb = reg(M + 10 * KB, 128, BF16)
    gexp_sb = reg(M + 3 * KB + 512, 128, F32)
    grec_sb = reg(M + 4 * KB, 16, F32)
    grec2_sb = reg(M + 4 * KB + 64, 16, F32)
    rsum_sb = reg(M + 4 * KB + 128, 512, BF16)
    bcast_sb = reg(M + 5 * KB + 256, 512, BF16)
    prod_sb = reg(M + 6 * KB + 512, 512, F32)
    grow_sb2 = [reg(M + 9 * KB, 512, BF16, parts=1),
                reg(M + 14 * KB, 512, BF16, parts=1)]

    ps_h = ctx.enter_context(nc.psum_tensor("ps", [128, 4096], F32))
    ps = ps_h[:, :]
    psb = ps.bitcast(BF16)

    def bank(b, cols=512, dtype=F32, parts=128, p0=0):
        if dtype == F32:
            return ps[p0:p0 + parts, b * 512: b * 512 + cols]
        return psb[p0:p0 + parts, b * 1024: b * 1024 + cols]

    sems = {}
    for name in ("sWL", "sXL", "sST", "sMV", "sPE", "sVE", "sSC", "sGP", "sVB"):
        sems[name] = ctx.enter_context(nc.semaphore(name))
    sch = Sched()
    for name in sems:
        sch.sem(name)
    E = sch.emit

    def tap(name, src_ap, waits):
        if not DEBUG or (TAPS is not None and name not in TAPS):
            return 0
        def fn(sy, name=name, src_ap=src_ap):
            return sy.dma_start(dbg[name], src_ap)
        return E("sync", fn, waits=waits, inc=("sST", 16))

    ev = E("gpsimd", lambda g: g.memset(ident, 0.0), inc=("sGP", 1))
    E("gpsimd", lambda g: g.affine_select(
        out=ident, in_=ident, compare_op=OP.not_equal, fill=1.0,
        base=0, pattern=[[-1, 128]], channel_multiplier=1),
        waits=[("sGP", ev)], inc=("sGP", 1))
    ev = E("gpsimd", lambda g: g.memset(onespad, 0.0), inc=("sGP", 1))
    E("gpsimd", lambda g: g.memset(onespad[:, 0:1], 1.0),
      waits=[("sGP", ev)], inc=("sGP", 1))
    E("gpsimd", lambda g: g.memset(ones1, 1.0), inc=("sGP", 1))
    E("gpsimd", lambda g: g.memset(negB, EXP_SHIFT), inc=("sGP", 1))
    E("gpsimd", lambda g: g.memset(eps_sb, EPS), inc=("sGP", 1))
    E("gpsimd", lambda g: g.memset(zero_sb, 0.0), inc=("sGP", 1))
    gp_consts = sch.cnt["sGP"]

    def wload(fn):
        return E("gpsimd", fn, inc=("sWL", 16))

    wl_wv = wload(lambda g: g.dma_start(
        wv_sb.rearrange("p (f o) -> p f o", f=NF),
        wv_d.rearrange("(f p) o -> p f o", p=128)))
    wl_w1 = wload(lambda g: g.dma_start(
        w1_sb.rearrange("p (e f h) -> p e f h", e=NE, f=NF), w1_d))
    wl_w2 = wload(lambda g: g.dma_start(
        w2_sb.rearrange("p (e o) -> p e o", e=NE), w2_d))
    wl_wg = wload(lambda g: g.dma_start(
        wg_sb.rearrange("p (f e) -> p f e", f=NF), wg_d))
    wl_b2 = wload(lambda g: g.dma_start(b2_sb, b2_d))
    wl_bq = wload(lambda g: g.dma_start(bqs, bq_d))
    wl_bq = wload(lambda g: g.dma_start(bks, bk_d))
    wl_bv = wload(lambda g: g.dma_start(bvs, bv_d))
    wl_b1 = wload(lambda g: g.dma_start(b1s, b1_d))
    wl_bg = wload(lambda g: g.dma_start(bgs, bg_d))

    wl_chunk = [0] * NF

    def load_chunk(j, waits):
        def fq(g, j=j):
            return g.dma_start(
                chnk[j % 2][0].rearrange("p (f o) -> p f o", f=NF),
                wqj_d[j].rearrange("(f p) o -> p f o", p=128))
        E("gpsimd", fq, waits=waits, inc=("sWL", 16))

        def fk(g, j=j):
            return g.dma_start(
                chnk[j % 2][1].rearrange("p (f o) -> p f o", f=NF),
                wkj_d[j].rearrange("(f p) o -> p f o", p=128))
        wl_chunk[j] = E("gpsimd", fk, inc=("sWL", 16))

    load_chunk(0, ())
    load_chunk(1, ())

    x_tiled = x_d.rearrange("(t p) d -> t p d", p=128)
    out_tiled = out_d.rearrange("(t p) d -> t p d", p=128)
    xl_ev = {}

    def load_x(t, waits):
        def fn(sy, t=t):
            return sy.dma_start(xroll[t % 2], x_tiled[t])
        xl_ev[t] = E("sync", fn, waits=waits, inc=("sXL", 16))

    load_x(0, ())
    load_x(1, ())

    ln_state = {"sqrt": [0] * 8, "slot": 0, "cons": [0] * 8}

    def emit_ln_stats(src_ap, waits):
        slot = ln_state["slot"]
        ln_state["slot"] = (slot + 1) % 8
        st_b, ag_b = stats_sb2[slot], aggr_sb2[slot]
        sd_b, rs_b = std_sb2[slot], rstd_sb2[slot]
        for cch in range(2):
            w = (list(waits) + [("sSC", ln_state["sqrt"][slot])]) if cch == 0 else ()
            E("vector", lambda v, s=src_ap, c=cch, st=st_b: v.bn_stats(
                st[:, 6 * c:6 * c + 6], s[:, 512 * c:512 * c + 512]),
                waits=w, inc=("sVB", 1))
        vev = E("vector", lambda v, ag=ag_b, st=st_b: v.bn_aggr(ag, st),
                waits=[("sVB", sch.cnt["sVB"])], inc=("sVE", 1))
        E("scalar", lambda s, ag=ag_b, sd=sd_b: s.activation(
            sd, ag[:, 1:2], ACT.Ln, bias=eps_sb),
            waits=[("sVE", vev), ("sGP", gp_consts),
                   ("sVE", ln_state["cons"][slot])], inc=("sSC", 1))
        sev = E("scalar", lambda s, sd=sd_b, rs=rs_b: s.activation(
            rs, sd, ACT.Exp, scale=-0.5, bias=zero_sb), inc=("sSC", 1))
        ln_state["sqrt"][slot] = sev

        def consumed(ev, slot=slot):
            ln_state["cons"][slot] = ev
        return sev, ag_b, rs_b, consumed

    pe_tp1 = [0, 0]
    ve_ev1 = [0, 0]
    ve_vev = [0, 0]
    xq = [xroll[0], xroll[1], oroll[0], oroll[1]]

    def emit_p1_tail(t, vap):
        par = t % 2
        for fc in range(NF):
            w = [("sVE", vap), ("sVE", ve_ev1[par]), ("sGP", gp_consts)] \
                if fc == 0 else ()
            pe_tp1[par] = E("tensor", lambda tt_, fc=fc, par=par: tt_.transpose(
                bank(par, 1024, BF16)[:, 128 * fc:128 * fc + 128],
                ln1tm[par][:, 128 * fc:128 * fc + 128], ident),
                waits=w, inc=("sPE", 1))
        dst = ln1_fm.rearrange("p (f s) -> p f s", f=NF)[:, :, 128 * t:128 * t + 128]
        ve_ev1[par] = E("vector", lambda v, dst=dst, par=par: v.tensor_copy(
            dst, bank(par, 1024, BF16).rearrange("p (f s) -> p f s", f=NF)),
            waits=[("sPE", pe_tp1[par])], inc=("sVE", 1))
        for oh in range(2):
            b = 2 + oh
            w0 = [("sVE", ve_ev1[par]), ("sWL", wl_wv), ("sVE", ve_vev[oh])]
            for fc in range(NF):
                stat = ln1_fm[:, fc * 2048 + 128 * t: fc * 2048 + 128 * t + 128]
                mov = wv_sb[:, fc * 1024 + 512 * oh: fc * 1024 + 512 * oh + 512]
                pe_v = E("tensor", lambda t_, s=stat, m=mov, b=b, fc=fc: t_.matmul(
                    bank(b), s, m, start=(fc == 0), stop=(fc == NF - 1)),
                    waits=w0 if fc == 0 else (), inc=("sPE", 1))
            dst = v_sb[:, t * 1024 + 512 * oh: t * 1024 + 512 * oh + 512]
            ve_vev[oh] = E("vector", lambda v, d=dst, b=b: v.tensor_copy(
                d, bank(b)), waits=[("sPE", pe_v)], inc=("sVE", 1))

    def load_x1(t, waits):
        def fn(sy, t=t):
            return sy.dma_start(xq[t % 4], x_tiled[t])
        xl_ev[t] = E("sync", fn, waits=waits, inc=("sXL", 16))
    load_x1(2, ())
    load_x1(3, ())
    for grp in range(NT // 4):
        ln_p1 = {}
        for q in range(4):
            t = 4 * grp + q
            ln_p1[q] = emit_ln_stats(xq[t % 4], [("sXL", xl_ev[t])])
        for q in range(4):
            t = 4 * grp + q
            par = t % 2
            rstd_ev, ag_a, rs_a, cons = ln_p1[q]
            vap = E("vector", lambda v, t=t, ag=ag_a, rs=rs_a: v.tensor_scalar(
                ln1tm[t % 2], xq[t % 4], ag[:, 0:1], rs,
                op0=OP.subtract, op1=OP.mult),
                waits=[("sSC", rstd_ev), ("sPE", pe_tp1[par])], inc=("sVE", 1))
            cons(vap)
            if t + 4 < NT:
                load_x1(t + 4, [("sVE", vap)])
            emit_p1_tail(t, vap)

    ln1_done = max(ve_ev1)
    v_done = max(ve_vev)

    att_exp_last = 0
    qkv_bank_ev = [ve_ev1[0], ve_ev1[1], ve_vev[0], ve_vev[1]]
    exp_hist = {}
    ve_pack_hist = {}
    ve_bc_hist = {}
    ve_prod_hist = {}
    pend = {"a": None, "b": None}
    gslot = 0
    att_done = 0

    def emit_pv(j, c, w, kc):
        p4 = kc % 4
        ob = 4 + (w % 2)
        sb_bank = 6 + (w % 2)
        for h in (0, 1):
            vstat = v_sb[:, kc * 1024 + 128 * j + 64 * h:
                         kc * 1024 + 128 * j + 64 * h + 64]
            mov = ptsb[p4][:, 512 * h:512 * h + 512]
            wts = [("sSC", exp_hist[j * 64 + c * 16 + kc])]
            if kc == 0 and w >= 2 and (w - 2) in ve_pack_hist:
                wts.append(("sVE", ve_pack_hist[w - 2]))
            E("tensor", lambda t_, vs=vstat, m=mov, h=h, kc=kc, ob=ob:
              t_.matmul(bank(ob, 512, F32, 64, 64 * h), vs, m,
                        start=(kc == 0), stop=(kc == NT - 1),
                        skip_group_check=True),
              waits=wts if h == 0 else (), inc=("sPE", 1))
        for h in (0, 1):
            mov = ptsb[p4][:, 512 * h:512 * h + 512]
            wts2 = []
            if h == 0 and kc == 0 and w >= 2 and (w - 2) in ve_bc_hist:
                wts2.append(("sVE", ve_bc_hist[w - 2]))
            E("tensor", lambda t_, m=mov, h=h, kc=kc, b=sb_bank:
              t_.matmul(bank(b, 512, F32, 64, 64 * h),
                        onespad[:, 0:64], m,
                        start=(kc == 0), stop=(kc == NT - 1),
                        skip_group_check=True),
              waits=wts2, inc=("sPE", 1))

    pe_rank_hist = {}

    def emit_norm_a(j, w, ve_recip):
        par = w % 2
        sb_b = 6 + par
        ob_b = 4 + par
        pe_b = E("tensor", lambda t_, b=sb_b: t_.matmul(
            bank(b, 512, F32, 64, 0), ones1[0:1, 0:64], rsum_sb[0:1, :],
            start=True, stop=True, skip_group_check=True),
            waits=[("sSC", ve_recip)], inc=("sPE", 1))
        pe_b = E("tensor", lambda t_, b=sb_b: t_.matmul(
            bank(b, 512, F32, 64, 64), ones1[64:65, 0:64], rsum_sb[64:65, :],
            start=True, stop=True, skip_group_check=True), inc=("sPE", 1))
        pe_rank_hist[w] = pe_b
        ve_bc = E("vector", lambda v, b=sb_b: v.tensor_copy(bcast_sb, bank(b)),
                  waits=[("sPE", pe_b)], inc=("sVE", 1))
        ve_bc_hist[w] = ve_bc
        vev = E("vector", lambda v, b=ob_b: v.scalar_tensor_tensor(
            prod_sb, bank(b), 1.0, bcast_sb, op0=OP.mult, op1=OP.mult),
            waits=[("sVE", ve_bc)], inc=("sVE", 1))
        ve_prod = E("vector", lambda v, j=j: v.tensor_scalar(
            bcast_sb, prod_sb, bvs[:, j:j + 1], None, op0=OP.add),
            waits=[("sVE", vev), ("sWL", wl_bv)], inc=("sVE", 1))
        ve_prod_hist[w] = ve_prod

    def emit_norm_b(j, c, w):
        par = w % 2
        ob_b = 4 + par
        pe_tp = 0
        for qt in range(4):
            wts = [("sVE", ve_prod_hist[w])] if qt == 0 else ()
            pe_tp = E("tensor", lambda t_, qt=qt, b=ob_b: t_.transpose(
                bank(b, 1024, BF16)[:, 128 * qt:128 * qt + 128],
                bcast_sb[:, 128 * qt:128 * qt + 128], ident),
                waits=wts, inc=("sPE", 1))
        dst = attnout.rearrange("p (t f) -> p t f", t=NT)[
            :, 4 * c:4 * c + 4, 128 * j:128 * j + 128]
        ve_pack_hist[w] = E("vector", lambda v, d=dst, b=ob_b: v.tensor_copy(
            d, bank(b, 512, BF16).rearrange("p (t f) -> p t f", t=4)),
            waits=[("sPE", pe_tp)], inc=("sVE", 1))

    for j in range(NF):
        for (pidx, bias_ap, dest) in ((0, bqs, q_sb), (1, bks, k_sb)):
            for c in range(4):
                b = (pidx * 4 + c) % 4
                w0 = [("sWL", wl_chunk[j]), ("sVE", ln1_done),
                      ("sVE", qkv_bank_ev[b]), ("sSC", att_exp_last)]
                for fc in range(NF):
                    stat = chnk[j % 2][pidx][:, fc * 128: fc * 128 + 128]
                    mov = ln1_fm[:, fc * 2048 + 512 * c: fc * 2048 + 512 * c + 512]
                    pe_ev = E("tensor", lambda t_, s=stat, m=mov, b=b, fc=fc:
                              t_.matmul(bank(b), s, m,
                                        start=(fc == 0), stop=(fc == NF - 1)),
                              waits=w0 if fc == 0 else (), inc=("sPE", 1))
                dst = dest[:, 512 * c: 512 * c + 512]
                qkv_bank_ev[b] = E("vector", lambda v, d=dst, b=b, ba=bias_ap, j=j:
                                   v.tensor_scalar(d, bank(b), ba[:, j:j + 1], None,
                                                   op0=OP.add),
                                   waits=[("sPE", pe_ev), ("sWL", wl_bq)],
                                   inc=("sVE", 1))
            if pidx == 0 and pend["a"] is not None:
                pend["a"]()
                pend["a"] = None
            if pidx == 1 and pend["b"] is not None:
                pend["b"]()
                pend["b"] = None
        qk_done = max(qkv_bank_ev)
        if j + 2 < NF:
            load_chunk(j + 2, [("sPE", pe_ev)])

        for c in range(4):
            w = j * 4 + c
            par_w = w % 2
            ob = 4 + par_w
            sb_bank = 6 + par_w
            q0 = 512 * c
            prev_slot = {}
            for kc in range(NT):
                par = kc % 2
                pb = 2 * par
                g = gslot
                gslot += 1
                wts = [("sVE", qk_done)]
                if g - 2 in exp_hist:
                    wts.append(("sSC", exp_hist[g - 2]))
                if j == 0 and c == 0 and kc < 2:
                    wts.append(("sVE", v_done))
                for h in (0, 1):
                    r0 = 64 * h
                    stat = k_sb[r0:r0 + 64, 128 * kc:128 * kc + 128]
                    mov = q_sb[r0:r0 + 64, q0:q0 + 512]
                    pe_st = E("tensor", lambda t_, s=stat, m=mov, pb=pb, h=h:
                              t_.matmul(bank(pb + h), s, m, start=True, stop=True),
                              waits=wts if h == 0 else (), inc=("sPE", 1))
                if kc == 4 and pend["a"] is not None:
                    pend["a"]()
                    pend["a"] = None
                if kc == 12 and pend["b"] is not None:
                    pend["b"]()
                    pend["b"] = None
                if kc > 1:
                    emit_pv(j, c, w, kc - 2)
                sc_ev = E("scalar", lambda s_, pb=pb, p4=kc % 4: s_.activation(
                    ptsb[p4], bank(pb, 1024), ACT.Exp, bias=negB, scale=0.125),
                    waits=[("sPE", pe_st), ("sGP", gp_consts)], inc=("sSC", 1))
                exp_hist[g] = sc_ev
                att_exp_last = sc_ev
            emit_pv(j, c, w, NT - 2)
            emit_pv(j, c, w, NT - 1)
            pe_sums_last = sch.cnt["sPE"]

            wr = [("sPE", pe_sums_last)]
            if (w - 1) in pe_rank_hist:
                wr.append(("sPE", pe_rank_hist[w - 1]))
            if (w - 1) in ve_prod_hist:
                wr.append(("sVE", ve_prod_hist[w - 1]))

            def f_ln_a(s, b=sb_bank):
                return s.activation(prod_sb[0:1, :], bank(b, 512, F32, 1, 0),
                                    ACT.Ln, bias=zero_sb[0:1, :])

            def f_ex_a(s):
                with nc.allow_low_precision(reason="rsum feeds bf16 matmul"):
                    return s.activation(rsum_sb[0:1, :], prod_sb[0:1, :],
                                        ACT.Exp, scale=-1.0, bias=zero_sb[0:1, :])

            def f_ln_b(s, b=sb_bank):
                return s.activation(prod_sb[64:65, :], bank(b, 512, F32, 1, 64),
                                    ACT.Ln, bias=zero_sb[64:65, :])

            def f_ex_b(s):
                with nc.allow_low_precision(reason="rsum feeds bf16 matmul"):
                    return s.activation(rsum_sb[64:65, :], prod_sb[64:65, :],
                                        ACT.Exp, scale=-1.0, bias=zero_sb[64:65, :])
            E("scalar", f_ln_a, waits=wr, inc=("sSC", 1))
            E("scalar", f_ex_a, inc=("sSC", 1))
            E("scalar", f_ln_b, inc=("sSC", 1))
            ve_recip = E("scalar", f_ex_b, inc=("sSC", 1))

            def mk_pend(j=j, c=c, w=w, ve_recip=ve_recip):
                def fa():
                    emit_norm_a(j, w, ve_recip)

                def fb():
                    emit_norm_b(j, c, w)
                return fa, fb
            pend["a"], pend["b"] = mk_pend()
    pend["a"]()
    pend["b"]()
    att_done = ve_pack_hist[NF * 4 - 1]

    tap("d_att", attnout, [("sVE", att_done)])

    tp_ev = [0, 0]
    tp_pe = [0, 0]
    tp_next = [0]

    def load_x4(t, waits):
        def fn(sy, t=t):
            return sy.dma_start(xroll[t % 2], x_tiled[t])
        xl_ev[t] = E("sync", fn, waits=waits, inc=("sXL", 16))

    att_exp_final = att_exp_last
    load_x4(0, [("sVE", ve_pack_hist[28])])
    load_x4(1, [("sVE", ve_pack_hist[28])])

    ve_ln2win = [0, 0, 0, 0]
    ve_glog = 0
    ve_gfm_prev = 0
    gfm_evs = {}
    pe_rank = 0
    pe_rank_hist2 = {}
    pe_down_last = 0
    ve_hp = {}
    ve_dev = [0, 0]
    st_ev = [0] * NT
    pair_read = [0, 0]
    vadd_ev = {}
    ln_b_all = {}

    def preABC(c):
        t0 = 4 * c
        ln_a = {}
        for q in range(4):
            a_t = attnout[:, 1024 * (t0 + q):1024 * (t0 + q) + 1024]
            ln_a[q] = emit_ln_stats(a_t, [("sVE", ve_pack_hist[28 + c])])
        ln_b = {}
        for q in range(4):
            t = t0 + q
            a_t = attnout[:, 1024 * t:1024 * t + 1024]
            rstd_ev, ag_a, rs_a, cons = ln_a[q]
            vev = E("vector", lambda v, a=a_t, ag=ag_a, rs=rs_a:
                    v.tensor_scalar(
                lnt_sb, a, ag[:, 0:1], rs, op0=OP.subtract, op1=OP.mult),
                waits=[("sSC", rstd_ev)], inc=("sVE", 1))
            cons(vev)
            x2_t = x2_sb[:, 1024 * t:1024 * t + 1024]
            vadd = E("vector", lambda v, x2=x2_t, xp=t % 2: v.tensor_add(
                x2, xroll[xp], lnt_sb),
                waits=[("sVE", vev), ("sXL", xl_ev[t])], inc=("sVE", 1))
            if t + 2 < NT:
                load_x4(t + 2, [("sVE", vadd)])
            vadd_ev[t] = vadd
            ln_b[q] = emit_ln_stats(x2_t, [("sVE", vadd)])
        ln_b_all[c] = ln_b

    preABC(0)

    def emit_D_pair(c, pair):
        ln_b = ln_b_all[c]
        t0 = 4 * c
        par_of = {}

        def p4_d_ap(q):
            t = t0 + q
            par_of[q] = tp_next[0] % 2
            tp_next[0] += 1
            rstd_ev, ag_a, rs_a, cons = ln_b[q]
            x2_t = x2_sb[:, 1024 * t:1024 * t + 1024]
            vap = E("vector", lambda v, x2=x2_t, lp=t % 2, ag=ag_a, rs=rs_a:
                    v.tensor_scalar(
                ln1tm[lp], x2, ag[:, 0:1], rs,
                op0=OP.subtract, op1=OP.mult),
                waits=[("sSC", rstd_ev), ("sPE", tp_pe[par_of[q]])],
                inc=("sVE", 1))
            cons(vap)
            return vap

        def p4_d_tp(q, vap):
            t = t0 + q
            par = par_of[q]
            for fc in range(NF):
                w = [("sVE", vap), ("sVE", tp_ev[par]),
                     ("sSC", att_exp_final)] if fc == 0 else ()
                tp_pe[par] = E("tensor", lambda t_, fc=fc, par=par, lp=t % 2:
                               t_.transpose(
                    bank(par, 1024, BF16)[:, 128 * fc:128 * fc + 128],
                    ln1tm[lp][:, 128 * fc:128 * fc + 128], ident),
                    waits=w, inc=("sPE", 1))
            dst = ln2_fm.rearrange("p (f s) -> p f s", f=NF)[
                :, :, 128 * t:128 * t + 128]
            tp_ev[par] = E("vector", lambda v, dst=dst, par=par: v.tensor_copy(
                dst, bank(par, 1024, BF16).rearrange("p (f s) -> p f s", f=NF)),
                waits=[("sPE", tp_pe[par])], inc=("sVE", 1))
            ve_ln2win[c] = max(ve_ln2win[c], tp_ev[par])

        vaps = [p4_d_ap(2 * pair), p4_d_ap(2 * pair + 1)]
        p4_d_tp(2 * pair, vaps[0])
        p4_d_tp(2 * pair + 1, vaps[1])

    emit_D_pair(0, 0)
    for c in range(4):
        emit_D_pair(c, 1)

        w0g = [("sVE", ve_ln2win[c]), ("sWL", wl_wg), ("sVE", ve_gfm_prev)]
        if c == 0:
            w0g.append(("sVE", ve_pack_hist[30]))
        for fc in range(NF):
            stat = wg_sb[:, fc * 8: fc * 8 + 8]
            mov = ln2_fm[:, fc * 2048 + 512 * c: fc * 2048 + 512 * c + 512]
            pe_g = E("tensor", lambda t_, s=stat, m=mov, fc=fc: t_.matmul(
                bank(4, 512, F32, 8, 0), s, m, start=(fc == 0), stop=(fc == NF - 1)),
                waits=w0g if fc == 0 else (), inc=("sPE", 1))
        ve_glog = E("vector", lambda v: v.tensor_scalar(
            glog_win, bank(4, 512, F32, 8, 0),
            bgs, None, op0=OP.add),
            waits=[("sPE", pe_g), ("sWL", wl_bg)], inc=("sVE", 1))
        pe_h_pre = {}
        for e in (0, 1):
            hb = 2 + e % 2
            prior = ve_hp.get((e - 2) % NE, 0) if c > 0 else 0
            w0 = [("sVE", ve_ln2win[c]), ("sWL", wl_w1), ("sVE", prior),
                  ("sSC", att_exp_final)]
            if c > 0:
                w0.append(("sVE", pair_read[1]))
            for fc in range(NF):
                stat = w1_sb[:, (e * 8 + fc) * 128:(e * 8 + fc) * 128 + 128]
                mov = ln2_fm[:, fc * 2048 + 512 * c: fc * 2048 + 512 * c + 512]
                pe_h_pre[e] = E("tensor", lambda t_, s=stat, m=mov, hb=hb, fc=fc:
                                t_.matmul(
                    bank(hb), s, m, start=(fc == 0), stop=(fc == NF - 1)),
                    waits=w0 if fc == 0 else (), inc=("sPE", 1))
        pe_gt = 0
        for q in range(4):
            cc = 4 * c + q
            w = [("sVE", ve_glog)] if q == 0 else ()
            pe_gt = E("tensor", lambda t_, q=q: t_.transpose(
                bank(4, 1024, BF16)[:, 512 + 8 * q:512 + 8 * q + 8],
                glog_win[:, 128 * q:128 * q + 128], ident[0:8, 0:8]),
                waits=w, inc=("sPE", 1))
        ve_gt = E("vector", lambda v, c=c: v.tensor_copy(
            gt_sb[:, 32 * c:32 * c + 32], bank(4, 1024, BF16)[:, 512:512 + 32]),
            waits=[("sPE", pe_gt)], inc=("sVE", 1))
        sc_g = E("scalar", lambda s, c=c: s.activation(
            gexp_sb[:, 32 * c:32 * c + 32], gt_sb[:, 32 * c:32 * c + 32],
            ACT.Exp, bias=zero_sb),
            waits=[("sVE", ve_gt)], inc=("sSC", 1))
        ve_gtm = 0
        for q in range(4):
            tt = 4 * c + q
            vev = E("vector", lambda v, tt=tt: v.tensor_scalar(
                gacc_sb, gexp_sb[:, 8 * tt:8 * tt + 8], 1.0, None,
                op0=OP.mult, op1=OP.add, accum_out=grec_sb[:, tt:tt + 1]),
                waits=[("sSC", sc_g)], inc=("sVE", 1))
            E("scalar", lambda s, tt=tt: s.activation(
                std_sb, grec_sb[:, tt:tt + 1], ACT.Ln, bias=zero_sb),
                waits=[("sVE", vev)], inc=("sSC", 1))
            scv = E("scalar", lambda s, tt=tt: s.activation(
                grec2_sb[:, tt:tt + 1], std_sb, ACT.Exp, scale=-1.0, bias=zero_sb),
                inc=("sSC", 1))
            ve_gtm = E("vector", lambda v, tt=tt: v.tensor_scalar(
                gt_sb[:, 8 * tt:8 * tt + 8], gexp_sb[:, 8 * tt:8 * tt + 8],
                grec2_sb[:, tt:tt + 1], None, op0=OP.mult),
                waits=[("sSC", scv)], inc=("sVE", 1))
        pe_gf = 0
        for q in range(4):
            tt = 4 * c + q
            w = [("sVE", ve_gtm)] if q == 0 else ()
            pe_gf = E("tensor", lambda t_, tt=tt, q=q: t_.transpose(
                bank(4, 1024, BF16, 8, 0)[:, 128 * q:128 * q + 128],
                gt_sb[:, 8 * tt:8 * tt + 8], ident),
                waits=w, inc=("sPE", 1))
        wgf = [("sPE", pe_gf)]
        if c > 0:
            wgf.append(("sMV", mv_grow))
        ve_gfm_prev = E("vector", lambda v: v.tensor_copy(
            gfm_win, bank(4, 1024, BF16, 8, 0)[:, 0:512]),
            waits=wgf, inc=("sVE", 1))
        gfm_evs[c] = ve_gfm_prev

        for e in range(NE):
            mv_grow = E("sync", lambda sy, e=e: sy.dma_start(
                grow_sb2[e % 2], gfm_win[e:e + 1, :]),
                waits=[("sVE", gfm_evs[c]),
                       ("sPE", pe_rank_hist2.get((e - 2) % NE, 0))],
                inc=("sMV", 16))
            hb = 2 + e % 2
            prior = ve_hp.get((e - 2) % NE, 0) if (e >= 2 or c > 0) else 0
            w0 = [("sVE", ve_ln2win[c]), ("sWL", wl_w1), ("sVE", prior),
                  ("sSC", att_exp_final)]
            if e < 2 and c > 0:
                w0.append(("sVE", pair_read[1]))
            for fc in range(NF):
                stat = w1_sb[:, (e * 8 + fc) * 128:(e * 8 + fc) * 128 + 128]
                mov = ln2_fm[:, fc * 2048 + 512 * c: fc * 2048 + 512 * c + 512]
                pe_h = E("tensor", lambda t_, s=stat, m=mov, hb=hb, fc=fc: t_.matmul(
                    bank(hb), s, m, start=(fc == 0), stop=(fc == NF - 1)),
                    waits=w0 if fc == 0 else (), inc=("sPE", 1))
            wr1 = [("sMV", mv_grow), ("sVE", prior)]
            if c == 0 and e == 0:
                wr1.append(("sVE", ve_pack_hist[31]))
            pe_rank = E("tensor", lambda t_, e=e: t_.matmul(
                bank(5, 512, F32, 128, 0), ones1[0:1, :], grow_sb2[e % 2],
                start=True, stop=True), waits=wr1, inc=("sPE", 1))
            pe_rank_hist2[e] = pe_rank
            vev = E("vector", lambda v, hb=hb, e=e: v.tensor_scalar(
                htmp_sb2[e % 2], bank(hb), b1s[:, e:e + 1], 0.0,
                op0=OP.add, op1=OP.max),
                waits=[("sPE", pe_h), ("sWL", wl_b1), ("sPE", pe_down_last)],
                inc=("sVE", 1))
            ve_hp[e] = E("vector", lambda v, e=e: v.scalar_tensor_tensor(
                hp_sb[e], htmp_sb2[e % 2], 1.0, bank(5, 512),
                op0=OP.mult, op1=OP.mult),
                waits=[("sVE", vev), ("sPE", pe_rank)], inc=("sVE", 1))
        down_done = {}

        def emit_down(tc):
            b0 = (6, 2)[tc % 2]
            for half in range(2):
                for e in range(NE):
                    wts = []
                    if half == 0:
                        if e == 0:
                            wts = [("sWL", wl_w2), ("sVE", ve_hp[0])]
                            if tc % 2 == 0:
                                if c == 0 and tc == 0:
                                    wts += [("sVE", ve_bc_hist[30]),
                                            ("sVE", ve_bc_hist[31])]
                                else:
                                    wts.append(("sVE", pair_read[0]))
                            else:
                                wts += [("sVE", ve_hp[6]), ("sVE", ve_hp[7]),
                                        ("sVE", pair_read[1])]
                        else:
                            wts = [("sVE", ve_hp[e])]
                    stat = hp_sb[e][:, 128 * tc:128 * tc + 128]
                    mov = w2_sb[:, e * 1024 + 512 * half:
                                e * 1024 + 512 * half + 512]
                    E("tensor", lambda t_, s=stat, m=mov, b=b0 + half, e=e:
                      t_.matmul(bank(b), s, m, start=(e == 0), stop=False),
                      waits=wts, inc=("sPE", 1))
                last = E("tensor", lambda t_, tc=tc, b=b0 + half, half=half:
                         t_.matmul(
                    bank(b), gfm_win[:, 128 * tc:128 * tc + 128],
                    b2_sb[:, 512 * half:512 * half + 512],
                    start=False, stop=True),
                    waits=[("sWL", wl_b2), ("sVE", gfm_evs[c])], inc=("sPE", 1))
            down_done[tc] = last

        def emit_p6(tc):
            t = 4 * c + tc
            view = bank((6, 2)[tc % 2], 1024, F32)
            rstd_ev, ag_a, rs_a, cons = emit_ln_stats(
                view, [("sPE", down_done[tc])])
            vev = E("vector", lambda v, view=view, ag=ag_a, rs=rs_a:
                    v.tensor_scalar(
                lnt_sb, view, ag[:, 0:1], rs, op0=OP.subtract, op1=OP.mult),
                waits=[("sSC", rstd_ev)], inc=("sVE", 1))
            cons(vev)
            pair_read[tc % 2] = vev
            x2_t = x2_sb[:, 1024 * t:1024 * t + 1024]
            w = [("sVE", vev)]
            if t >= 2:
                w.append(("sST", st_ev[t - 2]))
            vout = E("vector", lambda v, x2=x2_t, op=t % 2: v.tensor_add(
                oroll[op], x2, lnt_sb), waits=w, inc=("sVE", 1))
            st_ev[t] = E("sync", lambda sy, t=t, op=t % 2: sy.dma_start(
                out_tiled[t], oroll[op]), waits=[("sVE", vout)], inc=("sST", 16))

        emit_down(0)
        emit_down(1)
        emit_p6(0)
        emit_down(2)
        emit_p6(1)
        emit_down(3)
        pe_down_last = down_done[3]
        if c + 1 < 4:
            preABC(c + 1)
            emit_D_pair(c + 1, 0)
        emit_p6(2)
        emit_p6(3)
    final_st = sch.cnt["sST"]
    E("sync", lambda sy: sy.wait_ge(sems["sST"], final_st), waits=())

    block = ctx.enter_context(nc.Block())

    def mk(name):
        def body(eng):
            run_engine(eng, name, sch.ops[name], sems)
        return body

    block.sync(mk("sync"))
    block.tensor(mk("tensor"))
    block.vector(mk("vector"))
    block.scalar(mk("scalar"))
    block.gpsimd(mk("gpsimd"))
    ctx.close()
    return nc


def host_layouts(inputs):
    f32 = np.float32
    Wq = np.asarray(inputs["L0_attn_Wq"], f32)
    Wk = np.asarray(inputs["L0_attn_Wk"], f32)
    Wv = np.asarray(inputs["L0_attn_Wv"], f32)
    W1 = np.asarray(inputs["L0_ffn_W1"], f32)
    W2 = np.asarray(inputs["L0_ffn_W2"], f32)
    Wg = np.asarray(inputs["L0_ffn_Wg"], f32)
    d = {}
    wqT = np.ascontiguousarray(Wq.T)
    wkT = np.ascontiguousarray(Wk.T)
    d["wqj"] = np.ascontiguousarray(wqT.reshape(D, NF, 128).transpose(1, 0, 2))
    d["wkj"] = np.ascontiguousarray(wkT.reshape(D, NF, 128).transpose(1, 0, 2))
    d["wv"] = np.ascontiguousarray(Wv.T)
    d["w1l"] = np.ascontiguousarray(W1.reshape(NE, NF, 128, HD).transpose(2, 0, 1, 3))
    d["w2l"] = np.ascontiguousarray(W2.transpose(1, 0, 2))
    d["wgl"] = np.ascontiguousarray(Wg.reshape(NF, 128, NE).transpose(1, 0, 2))
    d["bq2"] = np.ascontiguousarray(np.asarray(inputs["L0_attn_bq"], f32).reshape(NF, 128).T)
    d["bk2"] = np.ascontiguousarray(np.asarray(inputs["L0_attn_bk"], f32).reshape(NF, 128).T)
    d["bv2"] = np.ascontiguousarray(np.asarray(inputs["L0_attn_bv"], f32).reshape(NF, 128).T)
    d["b12"] = np.ascontiguousarray(np.asarray(inputs["L0_ffn_b1"], f32).T)
    d["b2l"] = np.ascontiguousarray(np.asarray(inputs["L0_ffn_b2"], f32))
    d["bg2"] = np.ascontiguousarray(np.asarray(inputs["L0_ffn_bg"], f32).reshape(NE, 1))
    return d


_NC_CACHE = None


def kernel(**inputs):
    global _NC_CACHE
    if _NC_CACHE is None:
        _NC_CACHE = build_nc()
    nc = _NC_CACHE
    shared = host_layouts(inputs)
    x = np.asarray(inputs["L0_attn_x"], np.float32)
    in_maps = []
    for i in range(8):
        m = dict(shared)
        m["x"] = np.ascontiguousarray(x[i])
        in_maps.append(m)
    res = run_bass_kernel_spmd(nc, in_maps, core_ids=list(range(8)))
    if DEBUG:
        kernel.debug = res.results
    return np.stack([np.asarray(r["out"], np.float32) for r in res.results], axis=0)
